# version 23
# baseline (speedup 1.0000x reference)
import numpy as np
from contextlib import ExitStack

import concourse.bass as bass
import concourse.mybir as mybir
from concourse.bass_utils import run_bass_kernel_spmd

F32 = mybir.dt.float32
BF16 = mybir.dt.bfloat16
AX = mybir.AxisListType
AF = mybir.ActivationFunctionType
OP = mybir.AluOpType

S = 4096
D = 1024
NH = 16
PT = 128
TILE = 256
NT = S // TILE
NSB = S // PT
W1H = 4096
EPS = 1e-5
HEAD_OF = [0, 2, 1, 3]


class Sem:
    def __init__(self, handle):
        self.h = handle
        self.n = 0

    def inc(self, k=1):
        self.n += k
        return self.n


class Plan:
    ENGINES = ("sync", "gpsimd", "vector", "scalar", "tensor")

    def __init__(self):
        self.streams = {e: [] for e in self.ENGINES}
        self.last_wait = {}

    def emit(self, eng, fn):
        self.streams[eng].append(fn)

    def wait(self, eng, sem, val):
        if val <= 0:
            return
        key = (eng, id(sem))
        if self.last_wait.get(key, -1) >= val:
            return
        self.last_wait[key] = val
        self.emit(eng, lambda e, s=sem, v=val: e.wait_ge(s.h, v))


def build(nc, dbg=False):
    x_d = nc.dram_tensor("x", [S, D], F32, kind="ExternalInput").ap()
    Wq_d = nc.dram_tensor("Wq", [D, D], F32, kind="ExternalInput").ap()
    Wk_d = nc.dram_tensor("Wk", [D, D], F32, kind="ExternalInput").ap()
    Wv_d = nc.dram_tensor("Wv", [D, D], F32, kind="ExternalInput").ap()
    Wo_d = nc.dram_tensor("Wo", [D, D], F32, kind="ExternalInput").ap()
    bq_d = nc.dram_tensor("bq", [1, D], F32, kind="ExternalInput").ap()
    bv_d = nc.dram_tensor("bv", [1, D], F32, kind="ExternalInput").ap()
    bo_d = nc.dram_tensor("bo", [1, D], F32, kind="ExternalInput").ap()
    w1_d = nc.dram_tensor("w1", [D, W1H], F32, kind="ExternalInput").ap()
    b1_d = nc.dram_tensor("b1", [1, W1H], F32, kind="ExternalInput").ap()
    w2_d = nc.dram_tensor("w2", [W1H, D], F32, kind="ExternalInput").ap()
    b2_d = nc.dram_tensor("b2", [1, D], F32, kind="ExternalInput").ap()
    ident_d = nc.dram_tensor("ident", [128, 128], F32, kind="ExternalInput").ap()
    ident8_d = nc.dram_tensor("ident8", [128, 128], F32, kind="ExternalInput").ap()
    maskf_d = nc.dram_tensor("maskf", [128, 192], F32, kind="ExternalInput").ap()
    maskm_d = nc.dram_tensor("maskm", [128, 192], F32, kind="ExternalInput").ap()
    maskl_d = nc.dram_tensor("maskl", [128, 192], F32, kind="ExternalInput").ap()
    out_d = nc.dram_tensor("out", [S, D], F32, kind="ExternalOutput").ap()
    y_d = nc.dram_tensor("ybuf", [S, D], F32, kind="ExternalOutput" if dbg else "Internal").ap()

    es = ExitStack()
    E = es.enter_context

    w1b = E(nc.sbuf_tensor("w1b", [128, 8, W1H], BF16))
    ident = E(nc.sbuf_tensor("ident_sb", [128, 128], BF16))
    masks = E(nc.sbuf_tensor("masks", [128, 1, 192], BF16))
    ones_row = E(nc.sbuf_tensor("ones_row", [1, 256], F32))
    zrow = E(nc.sbuf_tensor("zrow", [1, 128], F32))
    bq_fm = E(nc.sbuf_tensor("bq_fm", [128, 8], F32))
    boe_repl = E(nc.sbuf_tensor("boe_repl", [128, D], F32))
    bv_fm = E(nc.sbuf_tensor("bv_fm", [128, 8], BF16))
    b1T = E(nc.sbuf_tensor("b1T", [128, 32], F32))
    st6 = E(nc.sbuf_tensor("st6", [128, 2, 12], F32))
    mv = E(nc.sbuf_tensor("mv", [128, 2, 2], F32))
    veps = E(nc.sbuf_tensor("veps", [128, 2], F32))
    rstd = E(nc.sbuf_tensor("rstd", [128, 2], F32))
    nmr = E(nc.sbuf_tensor("nmr", [128, 2], F32))
    tmp_a = E(nc.sbuf_tensor("tmp_a", [128, 2], F32))
    tmp_b = E(nc.sbuf_tensor("tmp_b", [128, 2], F32))
    negmax = E(nc.sbuf_tensor("negmax", [128, 2, 2], F32))
    recip = E(nc.sbuf_tensor("recip", [128, 4], F32))

    es1 = ExitStack()
    E1 = es1.enter_context
    WqT = E1(nc.sbuf_tensor("WqT", [128, 8, D], BF16))
    WkT = E1(nc.sbuf_tensor("WkT", [128, 8, D], BF16))
    WvT = E1(nc.sbuf_tensor("WvT", [128, 8, D], BF16))
    WoT = E1(nc.sbuf_tensor("WoT", [128, 8, D], BF16))

    es1a = ExitStack()
    wstage = es1a.enter_context(nc.sbuf_tensor("wstage", [128, 8, D], BF16))
    bo_row = es1a.enter_context(nc.sbuf_tensor("bo_row", [1, D], F32))
    boe_row = es1a.enter_context(nc.sbuf_tensor("boe_row", [1, D], F32))

    P01 = E(nc.psum_tensor("P01", [128, 1024], F32))
    P23 = E(nc.psum_tensor("P23", [128, 1024], F32))
    P45 = E(nc.psum_tensor("P45", [128, 1024], F32))
    P67 = E(nc.psum_tensor("P67", [128, 1024], F32))

    s_x = [Sem(E(nc.semaphore("s_x0"))), Sem(E(nc.semaphore("s_x1")))]
    s_xr = [Sem(E(nc.semaphore("s_xr0"))), Sem(E(nc.semaphore("s_xr1")))]
    s_yl = [Sem(E(nc.semaphore("s_yl0"))), Sem(E(nc.semaphore("s_yl1")))]
    s_ys = [Sem(E(nc.semaphore("s_ys0"))), Sem(E(nc.semaphore("s_ys1")))]
    s_os = [Sem(E(nc.semaphore("s_os0"))), Sem(E(nc.semaphore("s_os1")))]
    s_wst = Sem(E(nc.semaphore("s_wst")))
    s_cst = Sem(E(nc.semaphore("s_cst")))
    s_w1 = Sem(E(nc.semaphore("s_w1")))
    s_w2 = Sem(E(nc.semaphore("s_w2")))
    s_bv = Sem(E(nc.semaphore("s_bv")))
    s_b1g = Sem(E(nc.semaphore("s_b1g")))
    s_b2g = Sem(E(nc.semaphore("s_b2g")))
    s_g = Sem(E(nc.semaphore("s_g")))
    s_te = Sem(E(nc.semaphore("s_te")))
    s_ve = Sem(E(nc.semaphore("s_ve")))
    s_se = Sem(E(nc.semaphore("s_se")))

    P = Plan()
    marks = {}

    def gload(name, dst_ap, src_ap, sem=None, slow=False):
        P.wait("gpsimd", sem, sem.n)
        P.emit("gpsimd", lambda e, d=dst_ap, s=src_ap, sl=slow, sm=sem: e.dma_start(
            out=d, in_=s, allow_slow_non_contiguous=sl).then_inc(sm.h, 16))
        marks[name] = (sem, sem.inc(16))

    def sload(name, dst_ap, src_ap, sem=None, slow=False):
        P.wait("sync", sem, sem.n)
        P.emit("sync", lambda e, d=dst_ap, s=src_ap, sl=slow, sm=sem: e.dma_start(
            out=d, in_=s, allow_slow_non_contiguous=sl).then_inc(sm.h, 16))
        marks[name] = (sem, sem.inc(16))

    def inc_of(eng):
        return {"vector": s_ve, "scalar": s_se, "tensor": s_te}[eng]

    def emit_ginc(fn):
        P.emit("gpsimd", lambda e, f=fn: f(e).then_inc(s_g.h, 1))
        return s_g.inc()

    def emit_inc(eng, fn):
        sem = inc_of(eng)
        P.emit(eng, lambda e, f=fn, s=sem: f(e).then_inc(s.h, 1))
        return sem.inc()

    gload("ident", ident[:], ident_d, sem=s_cst)
    gload("maskadd", masks[:, 0, :], maskf_d, sem=s_cst)

    gload("bqf", bq_fm[:], bq_d.rearrange("o (c p) -> p (o c)", p=128), sem=s_bv, slow=True)
    sload("bo", bo_row[:], bo_d, sem=s_b1g)
    gload("bv", bv_fm[:], bv_d.rearrange("o (c p) -> p (o c)", p=128), sem=s_bv, slow=True)

    ve_ones = emit_inc("vector", lambda e: e.memset(ones_row[:], 1.0))
    ve_zrow = emit_inc("vector", lambda e: e.memset(zrow[:], 0.0))

    P.wait("scalar", s_bv, 32)
    se_bq = emit_inc("scalar", lambda e: e.mul(bq_fm[:], bq_fm[:], 0.125))

    wt_evac_mark = 0
    for wi, (nm, src, dstT, wsc) in enumerate((
            ("Wq", Wq_d, WqT, 0.125), ("Wk", Wk_d, WkT, 1.0),
            ("Wv", Wv_d, WvT, 1.0), ("Wo", Wo_d, WoT, 1.0))):
        if wi > 0:
            P.wait("gpsimd", s_se, wt_evac_mark)
        gload(nm, wstage[:], src.rearrange("(c p) d -> p c d", p=128), sem=s_wst)
        P.wait("tensor", s_wst, marks[nm][1])
        P.wait("tensor", s_cst, 32)
        for c in range(8):
            for j in range(8):
                half, off = j // 4, (j % 4) * 128
                fn = (lambda e, c=c, j=j, half=half, off=off: e.matmul(
                    P01[:, half * 512 + off: half * 512 + off + 128],
                    wstage[:, c, j * 128:(j + 1) * 128], ident[:],
                    start=True, stop=True))
                if j == 7:
                    t_mark = emit_inc("tensor", fn)
                else:
                    P.emit("tensor", fn)
            P.wait("scalar", s_te, t_mark)
            for j in range(8):
                half, off = j // 4, (j % 4) * 128
                fn = (lambda e, c=c, j=j, half=half, off=off, d=dstT, wsc=wsc: e.activation(
                    d[:, j, c * 128:(c + 1) * 128],
                    P01[:, half * 512 + off: half * 512 + off + 128],
                    AF.Identity, scale=wsc))
                if j == 7:
                    wt_evac_mark = emit_inc("scalar", fn)
                else:
                    P.emit("scalar", fn)
            P.wait("tensor", s_se, wt_evac_mark)
    se_wt_all = wt_evac_mark

    gload("w1", w1b[:], w1_d.rearrange("(c p) h -> p c h", p=128), sem=s_w1)
    sload("b1", b1T[:], b1_d.rearrange("o (c p) -> p (o c)", p=128), sem=s_b1g, slow=True)

    P.wait("tensor", s_bv, 32)
    P.wait("tensor", s_b1g, 16)
    P.wait("tensor", s_ve, ve_ones)
    P.wait("tensor", s_ve, ve_zrow)
    for half in range(2):
        P.emit("tensor", lambda e, h=half: e.matmul(
            P23[0:1, h * 512:(h + 1) * 512], ones_row[0:1, 0:1],
            bo_row[0:1, h * 512:(h + 1) * 512], start=True, stop=False))
        for c in range(8):
            fn = (lambda e, h=half, c=c: e.matmul(
                P23[0:1, h * 512:(h + 1) * 512], bv_fm[:, c:c + 1],
                WoT[:, c, h * 512:(h + 1) * 512], start=False, stop=(c == 7)))
            if half == 1 and c == 7:
                te_boe = emit_inc("tensor", fn)
            else:
                P.emit("tensor", fn)
    P.wait("vector", s_te, te_boe)
    ve_boe0 = emit_inc("vector", lambda e: e.tensor_copy(boe_row[0:1, :], P23[0:1, 0:1024]))
    P.wait("tensor", s_ve, ve_boe0)
    P.wait("tensor", s_cst, 32)
    for half in range(2):
        fn = (lambda e, h=half: e.matmul(
            P23[:, h * 512:(h + 1) * 512], ones_row[0:1, 0:128],
            boe_row[0:1, h * 512:(h + 1) * 512], start=True, stop=True))
        if half == 1:
            te_boe2 = emit_inc("tensor", fn)
        else:
            P.emit("tensor", fn)
    P.wait("vector", s_te, te_boe2)
    ve_boe = emit_inc("vector", lambda e: e.tensor_copy(boe_repl[:], P23[:, 0:1024]))

    es1a.close()
    es1b = ExitStack()
    E1b = es1b.enter_context
    x_ln = E1b(nc.sbuf_tensor("x_ln", [128, 2, D], F32))
    x_res = E1b(nc.sbuf_tensor("x_res", [128, 2, D], F32))
    qT = E1b(nc.sbuf_tensor("qT", [128, 2, 8, TILE], BF16))
    xnb = E1b(nc.sbuf_tensor("xnb", [128, D], BF16))
    xnT = E1b(nc.sbuf_tensor("xnT", [128, 2, 8, TILE + 32], BF16))
    kT = E1b(nc.sbuf_tensor("kT", [128, 2, 8, TILE], BF16))
    vsl = E1b(nc.sbuf_tensor("vsl", [128, 6, NH * 65], BF16))
    probs = E1b(nc.sbuf_tensor("probs", [128, 1, 4, 192], BF16))
    pTa = E1b(nc.sbuf_tensor("pTa", [128, 4, 128], BF16))
    pTb = E1b(nc.sbuf_tensor("pTb", [128, 4, 128], BF16))
    ctx_sb = E1b(nc.sbuf_tensor("ctx_sb", [128, 2, D], BF16))
    ao_sb = E1b(nc.sbuf_tensor("ao_sb", [128, 2, D], F32))
    ctxT = E1b(nc.sbuf_tensor("ctxT", [128, 8, 128], BF16))

    rstd_free = [(None, 0)]

    def ln_stats(xap, pt, wait_ldx=None):
        if wait_ldx is not None:
            P.wait("vector", wait_ldx[0], wait_ldx[1])
        emit_inc("vector", lambda e, pt=pt, xap=xap: e.bn_stats(st6[:, pt, 0:6], xap[:, 0:512]))
        m2 = emit_inc("vector", lambda e, pt=pt, xap=xap: e.bn_stats(st6[:, pt, 6:12], xap[:, 512:1024]))
        P.wait("vector", s_ve, m2)
        return emit_inc("vector", lambda e, pt=pt: e.bn_aggr(
            mv[:, pt, 0:2], st6.ap()[:, pt, 0:12].rearrange("p (k t) -> p k t", t=3)))

    def vstep(fn):
        m = emit_inc("vector", fn)
        P.wait("vector", s_ve, m)
        return m

    def ln_smalls():
        if rstd_free[0][0] is not None:
            P.wait("vector", rstd_free[0][0], rstd_free[0][1])
        P.wait("vector", s_ve, s_ve.n)
        m1 = vstep(lambda e: e.tensor_scalar_add(veps[:], mv[:, :, 1], EPS))
        P.wait("scalar", s_ve, m1)
        ml = emit_inc("scalar", lambda e: e.activation(tmp_a[:], veps[:], AF.Ln))
        P.wait("scalar", s_se, ml)
        m2 = emit_inc("scalar", lambda e: e.activation(rstd[:], tmp_a[:], AF.Exp, scale=-0.5))
        P.wait("vector", s_se, m2)
        return vstep(lambda e: e.scalar_tensor_tensor(
            nmr[:], mv[:, :, 0], -1.0, rstd[:], OP.mult, OP.mult))

    def ln_apply(dst_ap, src_ap, p):
        m = emit_inc("scalar", lambda e, d=dst_ap, s=src_ap, p=p: e.activation(
            d, s, AF.Identity, bias=nmr[:, p:p + 1], scale=rstd[:, p:p + 1]))
        rstd_free[0] = (s_se, m)
        return m

    def transpose8(src_sb, evac_eng, dst_fn, extra_te_wait=None):
        if extra_te_wait is not None:
            P.wait("tensor", *extra_te_wait)
        for j in range(8):
            half, off = j // 4, (j % 4) * 128
            fn = (lambda e, j=j, half=half, off=off, s=src_sb: e.matmul(
                P01[:, half * 512 + off: half * 512 + off + 128],
                s[:, j * 128:(j + 1) * 128], ident[:], start=True, stop=True))
            if j == 7:
                tmark = emit_inc("tensor", fn)
            else:
                P.emit("tensor", fn)
        P.wait(evac_eng, s_te, tmark)
        op = "copy" if evac_eng == "scalar" else "tensor_copy"
        fn1 = lambda e: getattr(e, op)(
            dst_fn(0), P01.ap()[:, 0:512].rearrange("p (j d) -> p j d", d=128))
        fn2 = lambda e: getattr(e, op)(
            dst_fn(1), P01.ap()[:, 512:1024].rearrange("p (j d) -> p j d", d=128))
        P.emit(evac_eng, fn1)
        m = emit_inc(evac_eng, fn2)
        P.wait("tensor", inc_of(evac_eng), m)
        return m

    P.wait("vector", s_se, se_wt_all)
    P.wait("vector", s_te, te_boe2)
    emit_inc("vector", lambda e: e.memset(kT[:], 0.0))
    m_z = emit_inc("vector", lambda e: e.memset(vsl[:], 0.0))
    P.wait("vector", s_ve, m_z)
    ve_memset = emit_inc("vector", lambda e: e.memset(
        vsl.ap().rearrange("p s (h c) -> p s h c", c=65)[:, :, :, 64:65], 1.0))

    xln_free = [0, 0]
    xres_free_ve = [(None, 0), (None, 0)]
    ao_free = [0, 0]
    wo_ev_marks = [0, 0]
    xres_free_st = [0, 0]
    xld = [[0, 0] for _ in range(NT)]
    xrld = [[0, 0] for _ in range(NT)]
    yst = [[0, 0] for _ in range(NT)]
    proj_v_done = [0] * (NT + 1)
    vmm_last = [0]
    kq_evac_last = [0]
    ln1_sm = [0]
    wo_mark_last = [0]
    xn_ovl_mark = [0, 0]

    A = {"exp": 0, "mask": 0, "pT": 0, "pTa_ev": 0, "pTb_ev": 0, "ctx_ev": 0, "wo_ev": 0, "gg": 0, "exp_pair": [0, 0], "mask_pair": [0, 0], "pT_pair": [0, 0], "pend": None, "ctx_mm": 0}

    def emit_projLNa(t):
        par = t % 2
        if t == 0:
            P.wait("gpsimd", s_se, se_wt_all)
            P.wait("gpsimd", s_te, te_boe2)
        for p in range(2):
            P.wait("gpsimd", s_se, xln_free[p])
            P.wait("gpsimd", s_x[p], s_x[p].n)
            P.emit("gpsimd", lambda e, t=t, p=p: e.dma_start(
                out=x_ln[:, p, :], in_=x_d[t * TILE + p * PT: t * TILE + (p + 1) * PT, :]
            ).then_inc(s_x[p].h, 16))
            xld[t][p] = s_x[p].inc(16)
        for p in range(2):
            ln_stats(x_ln[:, p, :], p, wait_ldx=(s_x[p], xld[t][p]))
        ln1_sm[0] = ln_smalls()

    def emit_projLNb(t):
        flush_pT_ctx()
        par = t % 2
        P.wait("scalar", s_ve, ln1_sm[0])
        P.wait("tensor", s_se, A["pTb_ev"])
        P.wait("tensor", s_se, A["pTa_ev"])
        P.wait("tensor", s_se, kq_evac_last[0])
        P.wait("scalar", s_te, vmm_last[0])
        if t > 0:
            xn_ovl_mark[par] = emit_inc("scalar", lambda e, par=par, prv=(t - 1) % 2: e.copy(
                xnT.ap()[:, par, :, 0:32], xnT.ap()[:, prv, :, TILE: TILE + 32]))

        for p in range(2):
            ap_mark = ln_apply(xnb[:], x_ln[:, p, :], p)
            xln_free[p] = ap_mark
            xnT_ev = transpose8(
                xnb, "scalar",
                lambda h, p=p, par=par: xnT[:, par, 4 * h: 4 * h + 4,
                                            32 + p * PT: 32 + (p + 1) * PT],
                extra_te_wait=(s_se, ap_mark))
        if t == 0:
            P.wait("scalar", s_se, xnT_ev)
            xn_ovl_mark[par] = emit_inc("scalar", lambda e, par=par: e.copy(
                xnT.ap()[:, par, :, 0:32], xnT.ap()[:, par, :, 32:64]))

    def emit_projMM(t):
        kbuf = t % 2
        par = t % 2
        if t == 0:
            P.wait("tensor", s_wst, marks["Wq"][1])
            P.wait("tensor", s_se, se_wt_all)
            P.wait("tensor", s_se, se_bq)
            P.wait("tensor", s_ve, ve_boe)
            P.wait("tensor", s_ve, ve_memset)

        P.wait("tensor", s_se, A["exp"])
        KQB = [(P01, 0), (P01, 512), (P23, 0), (P23, 512)]
        q_free = [0, 0, 0, 0]
        pend = A["pend"]
        A["pend"] = None
        for pi, ((WT_, isq), jorder) in enumerate((
                ((WqT, True), (2, 3, 6, 7)),
                ((WkT, False), (2, 3, 6, 7)),
                ((WqT, True), (0, 1, 4, 5)),
                ((WkT, False), (0, 1, 4, 5)))):
            if pi == 1 and pend is not None:
                pend_pT(pend[3])
                pend_pTa_ev()
                pend_pTb_ev()
            if pi == 2:
                if pend is not None:
                    pend_ctx(pend[0], pend[1], pend[2])
                    pend_ctx_ev(pend[0], pend[1])
                P.wait("tensor", s_se, A["pTa_ev"])
                P.wait("tensor", s_se, A["pTb_ev"])
            for j in jorder:
                h = j % 4
                PB, off = KQB[h]
                P.wait("tensor", s_se, q_free[h])
                for d in range(8):
                    fn = (lambda e, j=j, d=d, par=par, WT_=WT_, PB=PB, off=off: e.matmul(
                        PB[:, off: off + TILE], WT_[:, d, j * 128:(j + 1) * 128],
                        xnT[:, par, d, 32:32 + TILE], start=(d == 0), stop=(d == 7)))
                    if d == 7:
                        qmm = emit_inc("tensor", fn)
                    else:
                        P.emit("tensor", fn)
                P.wait("scalar", s_te, qmm)
                if isq:
                    q_free[h] = emit_inc("scalar", lambda e, j=j, par=par, PB=PB, off=off: e.activation(
                        qT[:, par, j, :], PB[:, off: off + TILE], AF.Identity,
                        bias=bq_fm[:, j:j + 1]))
                else:
                    P.wait("scalar", s_ve, ve_memset)
                    q_free[h] = emit_inc("scalar", lambda e, j=j, kb=kbuf, PB=PB, off=off: e.copy(
                        kT[:, kb, j, :], PB[:, off: off + TILE]))
        for h in range(4):
            P.wait("tensor", s_se, q_free[h])
        kq_evac_last[0] = max(q_free)

        P.wait("tensor", s_ve, A["ctx_ev"])
        P.wait("tensor", s_se, A["exp"])
        P.wait("tensor", s_se, xn_ovl_mark[t % 2])
        vev_last = 0
        slv = vsl.ap().rearrange("p s (h c) -> p s h c", c=65)
        par2 = t % 2
        slot_specs = [(2 * t, P23, 0), (2 * t + 1, P45, 128)]
        for sl, pt_, c0 in slot_specs:
            for half in range(2):
                for d in range(8):
                    fn = (lambda e, h=half, d=d, pt_=pt_, c0=c0, par=par2: e.matmul(
                        pt_[:, h * 512:(h + 1) * 512],
                        xnT[:, par, d, c0: c0 + 128],
                        WvT[:, d, h * 512:(h + 1) * 512],
                        start=(d == 0), stop=(d == 7)))
                    if half == 1 and d == 7:
                        vp_mark = emit_inc("tensor", fn)
                    else:
                        P.emit("tensor", fn)
            P.wait("vector", s_te, vp_mark)
            P.emit("vector", lambda e, sl=sl, pt_=pt_, slv=slv: e.tensor_copy(
                slv[:, sl % 6, 0:8, 0:64],
                pt_.ap()[:, 0:512].rearrange("p (h c) -> p h c", c=64)))
            vev_last = emit_inc("vector", lambda e, sl=sl, pt_=pt_, slv=slv: e.tensor_copy(
                slv[:, sl % 6, 8:16, 0:64],
                pt_.ap()[:, 512:1024].rearrange("p (h c) -> p h c", c=64)))
            P.wait("tensor", s_ve, vev_last)
        if t == NT - 1:
            P.wait("tensor", s_ve, vev_last)
            for half in range(2):
                for d in range(8):
                    fn = (lambda e, h=half, d=d, par=par2: e.matmul(
                        P23[0:32, h * 512:(h + 1) * 512],
                        xnT[:, par, d, TILE: TILE + 32],
                        WvT[:, d, h * 512:(h + 1) * 512],
                        start=(d == 0), stop=(d == 7)))
                    if half == 1 and d == 7:
                        vp_mark = emit_inc("tensor", fn)
                    else:
                        P.emit("tensor", fn)
            P.wait("vector", s_te, vp_mark)
            sl = 2 * t + 2
            P.emit("vector", lambda e, sl=sl, slv=slv: e.tensor_copy(
                slv[0:32, sl % 6, 0:8, 0:64],
                P23.ap()[0:32, 0:512].rearrange("p (h c) -> p h c", c=64)))
            vev_last = emit_inc("vector", lambda e, sl=sl, slv=slv: e.tensor_copy(
                slv[0:32, sl % 6, 8:16, 0:64],
                P23.ap()[0:32, 512:1024].rearrange("p (h c) -> p h c", c=64)))
            P.wait("tensor", s_ve, vev_last)
        vmm_last[0] = vp_mark
        proj_v_done[t] = vev_last

    def pend_pT(ppar):
        P.wait("tensor", s_se, A["exp"])
        P.wait("tensor", s_se, A["pTa_ev"])
        P.wait("tensor", s_se, A["pTb_ev"])
        for idx in range(4):
            fn = (lambda e, idx=idx, ppar=ppar: (
                e.matmul(P01[:, idx * 128:(idx + 1) * 128],
                         probs[:, ppar, idx, 0:128], ident[:], start=True, stop=True),
                e.matmul(P01[0:64, 512 + idx * 128: 512 + (idx + 1) * 128],
                         probs[:, ppar, idx, 128:192], ident[:], start=True, stop=True))[-1])
            if idx == 3:
                A["pT"] = emit_inc("tensor", fn)
            else:
                P.emit("tensor", fn)
        A["pT_pair"][ppar] = A["pT"]

    def pend_pTa_ev():
        P.wait("scalar", s_te, A["pT"])
        A["pTa_ev"] = emit_inc("scalar", lambda e: e.copy(
            pTa.ap().rearrange("p a b -> p (a b)"), P01[:, 0:512]))

    def pend_pTb_ev():
        P.wait("scalar", s_te, A["pT"])
        A["pTb_ev"] = emit_inc("scalar", lambda e: e.copy(
            pTb.ap()[0:64].rearrange("p a b -> p (a b)"), P01[0:64, 512:1024]))

    def pend_ctx(g, b, B):
        P.wait("tensor", s_se, A["pTa_ev"])
        P.wait("tensor", s_se, A["pTb_ev"])
        P.wait("tensor", s_ve, A["ctx_ev"])
        sl0, sl1 = B % 6, (B + 1) % 6
        for idx in range(4):
            hd_ = 4 * g + HEAD_OF[idx]
            fn = (lambda e, idx=idx, hd_=hd_, sl0=sl0, sl1=sl1: (
                e.matmul(P45[:, 512 + idx * 128: 512 + idx * 128 + 65],
                         pTa[:, idx, :], vsl[:, sl0, hd_ * 65: hd_ * 65 + 65],
                         start=True, stop=False),
                e.matmul(P45[:, 512 + idx * 128: 512 + idx * 128 + 65],
                         pTb[0:64, idx, :], vsl[0:64, sl1, hd_ * 65: hd_ * 65 + 65],
                         start=False, stop=True))[-1])
            if idx == 3:
                A["ctx_mm"] = emit_inc("tensor", fn)
            else:
                P.emit("tensor", fn)

    def pend_ctx_ev(g, b):
        P.wait("vector", s_te, A["ctx_mm"])
        m_rc = emit_inc("vector", lambda e: e.reciprocal(
            recip[:, 0:4],
            P45.ap()[:, 512:1024].rearrange("p (i c) -> p i c", c=128)[:, 0:4, 64:65]))
        P.wait("vector", s_ve, m_rc)
        cview = (ctx_sb.ap()[:, b, 256 * g: 256 * (g + 1)]
                 .rearrange("p (u v c) -> p v u c", v=2, c=64))
        psrc = (P45.ap()[:, 512:1024].rearrange("p (i c) -> p i c", c=128)[:, 0:4, 0:64]
                .rearrange("p (a b) c -> p a b c", a=2))
        A["ctx_ev"] = emit_inc("vector", lambda e, cv=cview, ps=psrc: e.tensor_tensor(
            cv, ps,
            recip.ap().rearrange("p (a b) -> p a b", a=2).unsqueeze(3)
            .broadcast_to([128, 2, 2, 64]), op=OP.mult))

    def flush_pT_ctx():
        if A["pend"] is None:
            return
        g, b, B, ppar = A["pend"]
        A["pend"] = None
        pend_pT(ppar)
        pend_pTa_ev()
        pend_pTb_ev()
        pend_ctx(g, b, B)
        pend_ctx_ev(g, b)

    def emit_attn_b(t, b):
        par_t = t % 2
        P.wait("tensor", s_ve, proj_v_done[min(t + b, NT - 1)])
        B = 2 * t + b
        g0 = A["gg"]
        A["gg"] += 4
        scmarks = {}

        def scores_of(g):
            gpar = (g0 + g) % 2
            SPP = P67 if gpar == 0 else P23
            P.wait("tensor", s_se, A["exp_pair"][gpar])
            if gpar == 0:
                P.wait("tensor", s_ve, A["wo_ev"])
            for idx in range(4):
                hd_ = 4 * g + HEAD_OF[idx]
                rb = (hd_ % 2) * 64
                pc = (hd_ % 2) * 512 + (idx % 2) * 192
                j = hd_ // 2
                qsl = (lambda rb=rb, j=j, b=b, par=par_t:
                       qT[rb:rb + 64, par, j, b * 128: b * 128 + 128])
                if b == 0:
                    kbp, kbc = (t - 1) % 2, t % 2
                    fn = (lambda e, rb=rb, pc=pc, j=j, kbp=kbp, kbc=kbc, q=qsl, SPP=SPP: (
                        e.matmul(SPP[:, pc: pc + 32], q(),
                                 kT[rb:rb + 64, kbp, j, 224:256], start=True, stop=True),
                        e.matmul(SPP[:, pc + 32: pc + 192], q(),
                                 kT[rb:rb + 64, kbc, j, 0:160], start=True, stop=True))[-1])
                else:
                    kbc, kbn = t % 2, (t + 1) % 2
                    fn = (lambda e, rb=rb, pc=pc, j=j, kbc=kbc, kbn=kbn, q=qsl, SPP=SPP: (
                        e.matmul(SPP[:, pc: pc + 160], q(),
                                 kT[rb:rb + 64, kbc, j, 96:256], start=True, stop=True),
                        e.matmul(SPP[:, pc + 160: pc + 192], q(),
                                 kT[rb:rb + 64, kbn, j, 0:32], start=True, stop=True))[-1])
                if idx == 3:
                    scmarks[g] = emit_inc("tensor", fn)
                else:
                    P.emit("tensor", fn)

        scores_of(0)
        for g in range(4):
            gpar = (g0 + g) % 2
            ppar = 0
            SPP = P67 if gpar == 0 else P23
            pend = A["pend"]
            A["pend"] = None
            if pend is not None:
                pend_pT(pend[3])
            P.wait("vector", s_te, scmarks[g])
            P.wait("vector", s_se, A["exp_pair"][gpar])
            if B == 0 or B == NSB - 1:
                c0_ = 0 if B == 0 else 160
                bm = 0
                for hh in range(2):
                    for ss in range(2):
                        off = hh * 512 + ss * 192 + c0_
                        bm = emit_inc("vector", lambda e, SPP=SPP, off=off: e.memset(
                            SPP[:, off: off + 32], -1e30))
                P.wait("vector", s_ve, bm)
            pm = 0
            for hh in range(2):
                pm = emit_inc("vector", lambda e, SPP=SPP, hh=hh: e.tensor_tensor(
                    SPP[:, hh * 512: hh * 512 + 384].rearrange("p (s c) -> p s c", c=192),
                    SPP[:, hh * 512: hh * 512 + 384].rearrange("p (s c) -> p s c", c=192),
                    masks.ap()[:, 0, :].unsqueeze(1).broadcast_to([128, 2, 192]),
                    op=OP.add))
            P.wait("vector", s_ve, pm)
            nm_mark = emit_inc("vector", lambda e, gpar=gpar, SPP=SPP: e.reduce_max(
                negmax[:, gpar, 0:1],
                SPP.ap().rearrange("p (h c) -> p h c", c=512)[:, :, 0:384],
                axis=AX.XY, negate=True))
            if g + 1 < 4:
                scores_of(g + 1)
            if pend is not None:
                pend_pTa_ev()
                pend_pTb_ev()
            P.wait("scalar", s_ve, nm_mark)
            P.wait("scalar", s_te, A["pT_pair"][ppar])
            A["exp"] = emit_inc("scalar", lambda e, gpar=gpar, SPP=SPP, ppar=ppar: e.activation(
                probs.ap()[:, 0, :, :].rearrange("p (a b) c -> p a (b c)", a=2),
                SPP.ap().rearrange("p (h c) -> p h c", c=512)[:, :, 0:384],
                AF.Exp, bias=negmax[:, gpar, 0:1]))
            A["exp_pair"][gpar] = A["exp"]
            if pend is not None:
                pend_ctx(pend[0], pend[1], pend[2])
            if pend is not None:
                pend_ctx_ev(pend[0], pend[1])
            A["pend"] = (g, b, B, ppar)

    def emit_postA(t):
        for p in range(2):
            if xres_free_ve[p][0] is not None:
                P.wait("gpsimd", xres_free_ve[p][0], xres_free_ve[p][1])
            P.wait("gpsimd", s_ys[p], xres_free_st[p])
            P.wait("gpsimd", s_xr[p], s_xr[p].n)
            P.emit("gpsimd", lambda e, t=t, p=p: e.dma_start(
                out=x_res[:, p, :], in_=x_d[t * TILE + p * PT: t * TILE + (p + 1) * PT, :]
            ).then_inc(s_xr[p].h, 16))
            xrld[t][p] = s_xr[p].inc(16)
        for p in range(2):
            P.wait("tensor", s_se, A["pTb_ev"])
            transpose8(
                ctx_sb.ap()[:, p, :], "scalar",
                lambda h: ctxT[:, 4 * h: 4 * h + 4, :],
                extra_te_wait=(s_ve, A["ctx_ev"]))
            if p == 0:
                P.wait("tensor", s_se, A["exp"])
            P.wait("tensor", s_ve, A["wo_ev"])
            for half in range(2):
                for j in range(8):
                    fn = (lambda e, h=half, j=j: e.matmul(
                        P67[:, h * 512:(h + 1) * 512], ctxT[:, j, :],
                        WoT[:, j, h * 512:(h + 1) * 512], start=(j == 0), stop=(j == 7)))
                    if half == 1 and j == 7:
                        wo_mark = emit_inc("tensor", fn)
                    else:
                        P.emit("tensor", fn)
            wo_mark_last[0] = wo_mark
            P.wait("vector", s_te, wo_mark)
            P.wait("vector", s_ve, ao_free[p])
            A["wo_ev"] = emit_inc("vector", lambda e, p=p: e.tensor_add(
                ao_sb[:, p, :], P67.ap()[:, 0:1024], boe_repl[:]))
            wo_ev_marks[p] = A["wo_ev"]
            if p == 0:
                flush_pT_ctx()

    def emit_postB(t):
        for p in range(2):
            ln_stats(ao_sb[:, p, :], p, wait_ldx=(s_ve, wo_ev_marks[p]))
        sm = ln_smalls()
        for p in range(2):
            P.wait("vector", s_ve, sm)
            P.wait("vector", s_xr[p], xrld[t][p])
            vstep(lambda e, p=p: e.tensor_scalar_add(
                x_res[:, p, :], x_res[:, p, :], nmr[:, p:p + 1]))
            y_mark = vstep(lambda e, p=p: e.scalar_tensor_tensor(
                x_res[:, p, :], ao_sb[:, p, :], rstd[:, p:p + 1], x_res[:, p, :],
                OP.mult, OP.add))
            rstd_free[0] = (s_ve, y_mark)
            xres_free_ve[p] = (s_ve, y_mark)
            ao_free[p] = y_mark
            P.wait("sync", s_ve, y_mark)
            P.wait("sync", s_ys[p], s_ys[p].n)
            P.emit("sync", lambda e, t=t, p=p: e.dma_start(
                out=y_d[t * TILE + p * PT: t * TILE + (p + 1) * PT, :], in_=x_res[:, p, :]
            ).then_inc(s_ys[p].h, 16))
            yst[t][p] = s_ys[p].inc(16)
            xres_free_st[p] = yst[t][p]

    emit_projLNa(0)
    emit_projLNb(0)
    emit_projMM(0)
    emit_projLNa(1)
    emit_projLNb(1)
    emit_projMM(1)
    emit_projLNa(2)
    emit_projLNb(2)
    emit_attn_b(0, 0)
    for k in range(NT):
        emit_attn_b(k, 1)
        emit_postA(k)
        if k + 1 < NT:
            emit_attn_b(k + 1, 0)
        emit_postB(k)
        if k + 3 < NT:
            emit_projLNa(k + 3)
        if k + 2 < NT:
            emit_projMM(k + 2)
        if k + 3 < NT:
            emit_projLNb(k + 3)

    es1b.close()
    es1.close()
    es2 = ExitStack()
    E2 = es2.enter_context
    w2b = E2(nc.sbuf_tensor("w2b", [128, 32, D], BF16))
    b2_row = E2(nc.sbuf_tensor("b2_row", [1, D], F32))
    b2_repl = E2(nc.sbuf_tensor("b2_repl", [128, D], F32))
    y_sb = E2(nc.sbuf_tensor("y_sb", [128, 2, 2, D], F32))
    ynb = E2(nc.sbuf_tensor("ynb", [128, D], BF16))
    ynT = E2(nc.sbuf_tensor("ynT", [128, 2, 8, TILE], BF16))
    hb = E2(nc.sbuf_tensor("hb", [128, 32, TILE], BF16))
    z_sb = E2(nc.sbuf_tensor("z_sb", [128, 2, D], F32))
    P.wait("gpsimd", s_se, xln_free[0])
    P.wait("gpsimd", s_se, xln_free[1])
    gload("b2", b2_row[:], b2_d, sem=s_b2g)
    P.wait("gpsimd", s_te, wo_mark_last[0])
    gload("w2", w2b[:], w2_d.rearrange("(c p) d -> p c d", p=128), sem=s_w2)

    y_free_ve = [[0, 0], [0, 0]]
    y_free_st = [[0, 0], [0, 0]]
    hb_free = [0]
    relu_marks = []

    P.wait("tensor", s_w1, 16)
    P.wait("tensor", s_b2g, 16)
    P.wait("scalar", s_b1g, 32)
    P.wait("vector", s_b1g, 32)
    P.wait("tensor", s_ve, A["ctx_ev"])
    P.wait("tensor", s_ve, A["wo_ev"])
    P.wait("tensor", s_se, s_se.n)
    for half in range(2):
        fn = (lambda e, h=half: e.matmul(
            P45[:, h * 512:(h + 1) * 512], ones_row[0:1, 0:128],
            b2_row[0:1, h * 512:(h + 1) * 512], start=True, stop=True))
        if half == 1:
            m_b2 = emit_inc("tensor", fn)
        else:
            P.emit("tensor", fn)
    P.wait("vector", s_te, m_b2)
    P.wait("vector", s_ys[0], s_ys[0].n)
    P.wait("vector", s_ys[1], s_ys[1].n)
    m_b2r = emit_inc("vector", lambda e: e.tensor_copy(b2_repl[:], P45[:, 0:1024]))
    P.wait("tensor", s_ve, m_b2r)

    yld = [[0, 0] for _ in range(NT)]
    ln3_sm = [0]
    ln3_marks = [0, 0]
    zev_prev = [0, 0]

    def p2_load(t):
        par = t % 2
        for p in range(2):
            P.wait("sync", s_ve, y_free_ve[par][p])
            P.wait("sync", s_os[p], y_free_st[par][p])
            P.wait("sync", s_ys[p], s_ys[p].n if t < 2 else yst[t][p])
            P.wait("sync", s_yl[p], s_yl[p].n)
            P.emit("sync", lambda e, t=t, p=p, par=par: e.dma_start(
                out=y_sb[:, par, p, :], in_=y_d[t * TILE + p * PT: t * TILE + (p + 1) * PT, :]
            ).then_inc(s_yl[p].h, 16))
            yld[t][p] = s_yl[p].inc(16)

    def p2_stats(t):
        par = t % 2
        for p in range(2):
            ln_stats(y_sb[:, par, p, :], p, wait_ldx=(s_yl[p], yld[t][p]))
        ln3_sm[0] = ln_smalls()

    def p2_trans_p(t, p):
        par = t % 2
        bank = P45 if p == 0 else P67
        P.wait("scalar", s_ve, ln3_sm[0])
        ap_mark = ln_apply(ynb[:], y_sb[:, par, p, :], p)
        P.wait("tensor", s_ve, zev_prev[p])
        P.wait("tensor", s_se, ln3_marks[p])
        P.wait("tensor", s_se, ap_mark)
        for j in range(8):
            half, off = j // 4, (j % 4) * 128
            fn = (lambda e, j=j, half=half, off=off, bank=bank: e.matmul(
                bank[:, half * 512 + off: half * 512 + off + 128],
                ynb[:, j * 128:(j + 1) * 128], ident[:], start=True, stop=True))
            if j == 7:
                tmark = emit_inc("tensor", fn)
            else:
                P.emit("tensor", fn)
        P.wait("scalar", s_te, tmark)
        P.emit("scalar", lambda e, p=p, par=par, bank=bank: e.copy(
            ynT[:, par, 0:4, p * PT:(p + 1) * PT],
            bank.ap()[:, 0:512].rearrange("p (j d) -> p j d", d=128)))
        ln3_marks[p] = emit_inc("scalar", lambda e, p=p, par=par, bank=bank: e.copy(
            ynT[:, par, 4:8, p * PT:(p + 1) * PT],
            bank.ap()[:, 512:1024].rearrange("p (j d) -> p j d", d=128)))

    p2_load(0)
    p2_stats(0)
    p2_trans_p(0, 0)
    p2_trans_p(0, 1)
    P.wait("tensor", s_se, ln3_marks[0])
    P.wait("tensor", s_se, ln3_marks[1])
    for t in range(NT):
        par = t % 2
        region_free = [(s_te, 0), (s_te, 0)]
        for cpair in range(16):
            if t + 1 < NT:
                if cpair == 0:
                    p2_load(t + 1)
                elif cpair == 6:
                    p2_stats(t + 1)
                elif cpair == 10:
                    p2_trans_p(t + 1, 0)
                elif cpair == 12:
                    p2_trans_p(t + 1, 1)
            reg = cpair % 2
            ptr = P01 if reg == 0 else P23
            P.wait("tensor", *region_free[reg])
            for ci in range(2):
                c = 2 * cpair + ci
                for d in range(8):
                    fn = (lambda e, c=c, ci=ci, d=d, ptr=ptr, par=par: e.matmul(
                        ptr[:, ci * 512: ci * 512 + TILE], w1b[:, d, c * 128:(c + 1) * 128],
                        ynT[:, par, d, :], start=(d == 0), stop=(d == 7)))
                    if ci == 1 and d == 7:
                        h_mark = emit_inc("tensor", fn)
                    else:
                        P.emit("tensor", fn)
            c0 = 2 * cpair
            if cpair % 2 == 0:
                P.wait("scalar", s_te, h_mark)
                P.wait("scalar", s_te, hb_free[0])
                P.emit("scalar", lambda e, c0=c0, ptr=ptr: e.activation(
                    hb[:, c0, :], ptr[:, 0:TILE], AF.Relu, bias=b1T[:, c0:c0 + 1]))
                m = emit_inc("scalar", lambda e, c0=c0, ptr=ptr: e.activation(
                    hb[:, c0 + 1, :], ptr[:, 512:512 + TILE], AF.Relu, bias=b1T[:, c0 + 1:c0 + 2]))
                region_free[reg] = (s_se, m)
            else:
                P.wait("vector", s_te, h_mark)
                P.wait("vector", s_te, hb_free[0])
                P.emit("vector", lambda e, c0=c0, ptr=ptr: e.tensor_scalar(
                    hb[:, c0, :], ptr[:, 0:TILE], b1T[:, c0:c0 + 1], 0.0, OP.add, OP.max))
                m = emit_inc("vector", lambda e, c0=c0, ptr=ptr: e.tensor_scalar(
                    hb[:, c0 + 1, :], ptr[:, 512:512 + TILE], b1T[:, c0 + 1:c0 + 2], 0.0, OP.add, OP.max))
                region_free[reg] = (s_ve, m)
        P.wait("tensor", *region_free[0])
        P.wait("tensor", *region_free[1])
        P.wait("tensor", s_w2, 16)
        z_ev_marks = []
        for ts in range(2):
            preg = P45 if ts == 0 else P67
            P.wait("tensor", s_se, ln3_marks[ts])
            P.wait("tensor", s_ve, zev_prev[ts])
            for half in range(2):
                for c in range(32):
                    fn = (lambda e, ts=ts, h=half, c=c, preg=preg: e.matmul(
                        preg[:, h * 512:(h + 1) * 512], hb[:, c, ts * PT:(ts + 1) * PT],
                        w2b[:, c, h * 512:(h + 1) * 512], start=(c == 0), stop=(c == 31)))
                    if half == 1 and c == 31:
                        z_mark = emit_inc("tensor", fn)
                    else:
                        P.emit("tensor", fn)
            P.wait("vector", s_te, z_mark)
            if rstd_free[0][0] is not None:
                P.wait("vector", rstd_free[0][0], rstd_free[0][1])
            z_ev_marks.append(emit_inc("vector", lambda e, ts=ts, preg=preg: e.tensor_add(
                z_sb[:, ts, :], preg.ap()[:, 0:1024], b2_repl[:])))
        zev_prev[0], zev_prev[1] = z_ev_marks[0], z_ev_marks[1]
        hb_free[0] = s_te.n
        for p in range(2):
            ln_stats(z_sb[:, p, :], p, wait_ldx=(s_ve, z_ev_marks[p]))
        sm = ln_smalls()
        for p in range(2):
            P.wait("vector", s_ve, sm)
            vstep(lambda e, p=p, par=par: e.tensor_scalar_add(
                y_sb[:, par, p, :], y_sb[:, par, p, :], nmr[:, p:p + 1]))
            o_mark = vstep(lambda e, p=p, par=par: e.scalar_tensor_tensor(
                y_sb[:, par, p, :], z_sb[:, p, :], rstd[:, p:p + 1], y_sb[:, par, p, :],
                OP.mult, OP.add))
            rstd_free[0] = (s_ve, o_mark)
            y_free_ve[par][p] = o_mark
            P.wait("sync", s_ve, o_mark)
            P.wait("sync", s_os[p], s_os[p].n)
            P.emit("sync", lambda e, t=t, p=p, par=par: e.dma_start(
                out=out_d[t * TILE + p * PT: t * TILE + (p + 1) * PT, :], in_=y_sb[:, par, p, :]
            ).then_inc(s_os[p].h, 16))
            y_free_st[par][p] = s_os[p].inc(16)

    P.wait("sync", s_os[0], s_os[0].n)
    P.wait("sync", s_os[1], s_os[1].n)

    with nc.Block() as block:
        block.sync(lambda e: [f(e) for f in P.streams["sync"]])
        block.gpsimd(lambda e: [f(e) for f in P.streams["gpsimd"]])
        block.vector(lambda e: [f(e) for f in P.streams["vector"]])
        block.scalar(lambda e: [f(e) for f in P.streams["scalar"]])
        block.tensor(lambda e: [f(e) for f in P.streams["tensor"]])
    es2.close()
    es.close()
    return nc


def make_consts():
    ident = np.eye(128, dtype=np.float32)
    ident8 = (0.125 * np.eye(128)).astype(np.float32)
    i = np.arange(128)[:, None]
    j = np.arange(192)[None, :]
    band = ((j - i >= 0) & (j - i <= 64)).astype(np.float32)
    maskf = ((1.0 - band) * -1e30).astype(np.float32)
    maskl = band * (j < 160).astype(np.float32)
    return ident, ident8, maskf, band, maskl


def kernel(**inputs):
    x = np.ascontiguousarray(np.asarray(inputs["L0_attn_x"], dtype=np.float32))
    ident, ident8, maskf, maskm, maskl = make_consts()
    base = {
        "Wq": np.ascontiguousarray(np.asarray(inputs["L0_attn_Wq"], dtype=np.float32)),
        "Wk": np.ascontiguousarray(np.asarray(inputs["L0_attn_Wk"], dtype=np.float32)),
        "Wv": np.ascontiguousarray(np.asarray(inputs["L0_attn_Wv"], dtype=np.float32)),
        "Wo": np.ascontiguousarray(np.asarray(inputs["L0_attn_Wo"], dtype=np.float32)),
        "bq": np.ascontiguousarray(np.asarray(inputs["L0_attn_bq"], dtype=np.float32)).reshape(1, D),
        "bv": np.ascontiguousarray(np.asarray(inputs["L0_attn_bv"], dtype=np.float32)).reshape(1, D),
        "bo": np.ascontiguousarray(np.asarray(inputs["L0_attn_bo"], dtype=np.float32)).reshape(1, D),
        "w1": np.ascontiguousarray(np.asarray(inputs["L0_ffn_w1"], dtype=np.float32)),
        "b1": np.ascontiguousarray(np.asarray(inputs["L0_ffn_b1"], dtype=np.float32)).reshape(1, W1H),
        "w2": np.ascontiguousarray(np.asarray(inputs["L0_ffn_w2"], dtype=np.float32)),
        "b2": np.ascontiguousarray(np.asarray(inputs["L0_ffn_b2"], dtype=np.float32)).reshape(1, D),
        "ident": ident, "ident8": ident8,
        "maskf": maskf, "maskm": maskm, "maskl": maskl,
    }
    nc = bass.Bass("TRN2", target_bir_lowering=False)
    build(nc)
    in_maps = [dict(base, x=np.ascontiguousarray(x[b])) for b in range(8)]
    res = run_bass_kernel_spmd(nc, in_maps, core_ids=list(range(8)))
    return np.stack([np.asarray(r["out"]) for r in res.results], axis=0)


# version 25 (final)
# speedup vs baseline: 1.0631x; 1.0631x over previous
import numpy as np
from contextlib import ExitStack

import concourse.bass as bass
import concourse.mybir as mybir
from concourse.bass_utils import run_bass_kernel_spmd

F32 = mybir.dt.float32
BF16 = mybir.dt.bfloat16
AX = mybir.AxisListType
AF = mybir.ActivationFunctionType
OP = mybir.AluOpType

S = 4096
D = 1024
NH = 16
PT = 128
TILE = 256
NT = S // TILE
NSB = S // PT
W1H = 4096
EPS = 1e-5
HEAD_OF = [0, 2, 1, 3]


class Sem:
    def __init__(self, handle):
        self.h = handle
        self.n = 0

    def inc(self, k=1):
        self.n += k
        return self.n


class Plan:
    ENGINES = ("sync", "gpsimd", "vector", "scalar", "tensor")

    def __init__(self):
        self.streams = {e: [] for e in self.ENGINES}
        self.last_wait = {}

    def emit(self, eng, fn):
        self.streams[eng].append(fn)

    def wait(self, eng, sem, val):
        if val <= 0:
            return
        key = (eng, id(sem))
        if self.last_wait.get(key, -1) >= val:
            return
        self.last_wait[key] = val
        self.emit(eng, lambda e, s=sem, v=val: e.wait_ge(s.h, v))


def build(nc, dbg=False):
    x_d = nc.dram_tensor("x", [S, D], F32, kind="ExternalInput").ap()
    Wq_d = nc.dram_tensor("Wq", [D, D], F32, kind="ExternalInput").ap()
    Wk_d = nc.dram_tensor("Wk", [D, D], F32, kind="ExternalInput").ap()
    Wv_d = nc.dram_tensor("Wv", [D, D], F32, kind="ExternalInput").ap()
    Wo_d = nc.dram_tensor("Wo", [D, D], F32, kind="ExternalInput").ap()
    bq_d = nc.dram_tensor("bq", [1, D], F32, kind="ExternalInput").ap()
    bv_d = nc.dram_tensor("bv", [1, D], F32, kind="ExternalInput").ap()
    bo_d = nc.dram_tensor("bo", [1, D], F32, kind="ExternalInput").ap()
    w1_d = nc.dram_tensor("w1", [D, W1H], F32, kind="ExternalInput").ap()
    b1_d = nc.dram_tensor("b1", [1, W1H], F32, kind="ExternalInput").ap()
    w2_d = nc.dram_tensor("w2", [W1H, D], F32, kind="ExternalInput").ap()
    b2_d = nc.dram_tensor("b2", [1, D], F32, kind="ExternalInput").ap()
    ident_d = nc.dram_tensor("ident", [128, 128], F32, kind="ExternalInput").ap()
    ident8_d = nc.dram_tensor("ident8", [128, 128], F32, kind="ExternalInput").ap()
    maskf_d = nc.dram_tensor("maskf", [128, 192], F32, kind="ExternalInput").ap()
    maskm_d = nc.dram_tensor("maskm", [128, 192], F32, kind="ExternalInput").ap()
    maskl_d = nc.dram_tensor("maskl", [128, 192], F32, kind="ExternalInput").ap()
    out_d = nc.dram_tensor("out", [S, D], F32, kind="ExternalOutput").ap()
    y_d = nc.dram_tensor("ybuf", [S, D], F32, kind="ExternalOutput" if dbg else "Internal").ap()

    es = ExitStack()
    E = es.enter_context

    w1b = E(nc.sbuf_tensor("w1b", [128, 8, W1H], BF16))
    ident = E(nc.sbuf_tensor("ident_sb", [128, 128], BF16))
    masks = E(nc.sbuf_tensor("masks", [128, 1, 192], BF16))
    ones_row = E(nc.sbuf_tensor("ones_row", [1, 256], F32))
    zrow = E(nc.sbuf_tensor("zrow", [1, 128], F32))
    bq_fm = E(nc.sbuf_tensor("bq_fm", [128, 8], F32))
    boe_repl = E(nc.sbuf_tensor("boe_repl", [128, D], F32))
    bv_fm = E(nc.sbuf_tensor("bv_fm", [128, 8], BF16))
    b1T = E(nc.sbuf_tensor("b1T", [128, 32], F32))
    st6 = E(nc.sbuf_tensor("st6", [128, 2, 12], F32))
    mv = E(nc.sbuf_tensor("mv", [128, 2, 2], F32))
    veps = E(nc.sbuf_tensor("veps", [128, 2], F32))
    rstd = E(nc.sbuf_tensor("rstd", [128, 2], F32))
    nmr = E(nc.sbuf_tensor("nmr", [128, 2], F32))
    tmp_a = E(nc.sbuf_tensor("tmp_a", [128, 2], F32))
    tmp_b = E(nc.sbuf_tensor("tmp_b", [128, 2], F32))
    negmax = E(nc.sbuf_tensor("negmax", [128, 2, 2], F32))
    recip = E(nc.sbuf_tensor("recip", [128, 4], F32))

    es1 = ExitStack()
    E1 = es1.enter_context
    WqT = E1(nc.sbuf_tensor("WqT", [128, 8, D], BF16))
    WkT = E1(nc.sbuf_tensor("WkT", [128, 8, D], BF16))
    WvT = E1(nc.sbuf_tensor("WvT", [128, 8, D], BF16))
    WoT = E1(nc.sbuf_tensor("WoT", [128, 8, D], BF16))

    es1a = ExitStack()
    wstage = es1a.enter_context(nc.sbuf_tensor("wstage", [128, 2, 8, D], BF16))
    bo_row = es1a.enter_context(nc.sbuf_tensor("bo_row", [1, D], F32))
    boe_row = es1a.enter_context(nc.sbuf_tensor("boe_row", [1, D], F32))

    P01 = E(nc.psum_tensor("P01", [128, 1024], F32))
    P23 = E(nc.psum_tensor("P23", [128, 1024], F32))
    P45 = E(nc.psum_tensor("P45", [128, 1024], F32))
    P67 = E(nc.psum_tensor("P67", [128, 1024], F32))

    s_x = [Sem(E(nc.semaphore("s_x0"))), Sem(E(nc.semaphore("s_x1")))]
    s_xr = [Sem(E(nc.semaphore("s_xr0"))), Sem(E(nc.semaphore("s_xr1")))]
    s_yl = [Sem(E(nc.semaphore("s_yl0"))), Sem(E(nc.semaphore("s_yl1")))]
    s_ys = [Sem(E(nc.semaphore("s_ys0"))), Sem(E(nc.semaphore("s_ys1")))]
    s_os = [Sem(E(nc.semaphore("s_os0"))), Sem(E(nc.semaphore("s_os1")))]
    s_wst = Sem(E(nc.semaphore("s_wst")))
    s_wsy = Sem(E(nc.semaphore("s_wsy")))
    s_cst = Sem(E(nc.semaphore("s_cst")))
    s_w1 = Sem(E(nc.semaphore("s_w1")))
    s_w1y = Sem(E(nc.semaphore("s_w1y")))
    s_w2 = Sem(E(nc.semaphore("s_w2")))
    s_bv = Sem(E(nc.semaphore("s_bv")))
    s_b1g = Sem(E(nc.semaphore("s_b1g")))
    s_b2g = Sem(E(nc.semaphore("s_b2g")))
    s_g = Sem(E(nc.semaphore("s_g")))
    s_te = Sem(E(nc.semaphore("s_te")))
    s_ve = Sem(E(nc.semaphore("s_ve")))
    s_se = Sem(E(nc.semaphore("s_se")))

    P = Plan()
    marks = {}

    def gload(name, dst_ap, src_ap, sem=None, slow=False):
        P.wait("gpsimd", sem, sem.n)
        P.emit("gpsimd", lambda e, d=dst_ap, s=src_ap, sl=slow, sm=sem: e.dma_start(
            out=d, in_=s, allow_slow_non_contiguous=sl).then_inc(sm.h, 16))
        marks[name] = (sem, sem.inc(16))

    def sload(name, dst_ap, src_ap, sem=None, slow=False):
        P.wait("sync", sem, sem.n)
        P.emit("sync", lambda e, d=dst_ap, s=src_ap, sl=slow, sm=sem: e.dma_start(
            out=d, in_=s, allow_slow_non_contiguous=sl).then_inc(sm.h, 16))
        marks[name] = (sem, sem.inc(16))

    def inc_of(eng):
        return {"vector": s_ve, "scalar": s_se, "tensor": s_te}[eng]

    def emit_ginc(fn):
        P.emit("gpsimd", lambda e, f=fn: f(e).then_inc(s_g.h, 1))
        return s_g.inc()

    def emit_inc(eng, fn):
        sem = inc_of(eng)
        P.emit(eng, lambda e, f=fn, s=sem: f(e).then_inc(s.h, 1))
        return sem.inc()

    gload("ident", ident[:], ident_d, sem=s_cst)
    gload("maskadd", masks[:, 0, :], maskf_d, sem=s_cst)

    gload("bqf", bq_fm[:], bq_d.rearrange("o (c p) -> p (o c)", p=128), sem=s_bv, slow=True)
    sload("bo", bo_row[:], bo_d, sem=s_b1g)
    gload("bv", bv_fm[:], bv_d.rearrange("o (c p) -> p (o c)", p=128), sem=s_bv, slow=True)

    ve_ones = emit_inc("vector", lambda e: e.memset(ones_row[:], 1.0))
    ve_zrow = emit_inc("vector", lambda e: e.memset(zrow[:], 0.0))

    P.wait("scalar", s_bv, 32)
    se_bq = emit_inc("scalar", lambda e: e.mul(bq_fm[:], bq_fm[:], 0.125))

    wt_evac_mark = 0
    te_done = {}
    for wi, (nm, src, dstT, wsc) in enumerate((
            ("Wq", Wq_d, WqT, 0.125), ("Wk", Wk_d, WkT, 1.0),
            ("Wv", Wv_d, WvT, 1.0), ("Wo", Wo_d, WoT, 1.0))):
        buf = wi % 2
        if wi >= 2:
            P.wait("gpsimd", s_te, te_done[wi - 2])
        P.emit("gpsimd", lambda e, b=buf, s=src: e.dma_start(
            out=wstage[:, b, :, :], in_=s.rearrange("(c p) d -> p c d", p=128)
        ).then_inc(s_wst.h, 16))
        marks[nm] = (s_wst, s_wst.inc(16))
        P.wait("tensor", s_wst, marks[nm][1])
        P.wait("tensor", s_cst, 32)
        for c in range(8):
            for j in range(8):
                half, off = j // 4, (j % 4) * 128
                fn = (lambda e, c=c, j=j, half=half, off=off, b=buf: e.matmul(
                    P01[:, half * 512 + off: half * 512 + off + 128],
                    wstage[:, b, c, j * 128:(j + 1) * 128], ident[:],
                    start=True, stop=True))
                if j == 7:
                    t_mark = emit_inc("tensor", fn)
                else:
                    P.emit("tensor", fn)
            P.wait("scalar", s_te, t_mark)
            for j in range(8):
                half, off = j // 4, (j % 4) * 128
                fn = (lambda e, c=c, j=j, half=half, off=off, d=dstT, wsc=wsc: e.activation(
                    d[:, j, c * 128:(c + 1) * 128],
                    P01[:, half * 512 + off: half * 512 + off + 128],
                    AF.Identity, scale=wsc))
                if j == 7:
                    wt_evac_mark = emit_inc("scalar", fn)
                else:
                    P.emit("scalar", fn)
            P.wait("tensor", s_se, wt_evac_mark)
        te_done[wi] = t_mark
    se_wt_all = wt_evac_mark

    gload("w1", w1b[:], w1_d.rearrange("(c p) h -> p c h", p=128), sem=s_w1)
    sload("b1", b1T[:], b1_d.rearrange("o (c p) -> p (o c)", p=128), sem=s_b1g, slow=True)

    P.wait("tensor", s_bv, 32)
    P.wait("tensor", s_b1g, 16)
    P.wait("tensor", s_ve, ve_ones)
    P.wait("tensor", s_ve, ve_zrow)
    for half in range(2):
        P.emit("tensor", lambda e, h=half: e.matmul(
            P23[0:1, h * 512:(h + 1) * 512], ones_row[0:1, 0:1],
            bo_row[0:1, h * 512:(h + 1) * 512], start=True, stop=False))
        for c in range(8):
            fn = (lambda e, h=half, c=c: e.matmul(
                P23[0:1, h * 512:(h + 1) * 512], bv_fm[:, c:c + 1],
                WoT[:, c, h * 512:(h + 1) * 512], start=False, stop=(c == 7)))
            if half == 1 and c == 7:
                te_boe = emit_inc("tensor", fn)
            else:
                P.emit("tensor", fn)
    P.wait("vector", s_te, te_boe)
    ve_boe0 = emit_inc("vector", lambda e: e.tensor_copy(boe_row[0:1, :], P23[0:1, 0:1024]))
    P.wait("tensor", s_ve, ve_boe0)
    P.wait("tensor", s_cst, 32)
    for half in range(2):
        fn = (lambda e, h=half: e.matmul(
            P23[:, h * 512:(h + 1) * 512], ones_row[0:1, 0:128],
            boe_row[0:1, h * 512:(h + 1) * 512], start=True, stop=True))
        if half == 1:
            te_boe2 = emit_inc("tensor", fn)
        else:
            P.emit("tensor", fn)
    P.wait("vector", s_te, te_boe2)
    ve_boe = emit_inc("vector", lambda e: e.tensor_copy(boe_repl[:], P23[:, 0:1024]))

    es1a.close()
    es1b = ExitStack()
    E1b = es1b.enter_context
    x_ln = E1b(nc.sbuf_tensor("x_ln", [128, 2, D], F32))
    x_res = E1b(nc.sbuf_tensor("x_res", [128, 2, D], F32))
    qT = E1b(nc.sbuf_tensor("qT", [128, 2, 8, TILE], BF16))
    xnb = E1b(nc.sbuf_tensor("xnb", [128, D], BF16))
    xnT = E1b(nc.sbuf_tensor("xnT", [128, 2, 8, TILE + 32], BF16))
    kT = E1b(nc.sbuf_tensor("kT", [128, 2, 8, TILE], BF16))
    vsl = E1b(nc.sbuf_tensor("vsl", [128, 6, NH * 65], BF16))
    probs = E1b(nc.sbuf_tensor("probs", [128, 1, 4, 192], BF16))
    pTa = E1b(nc.sbuf_tensor("pTa", [128, 4, 128], BF16))
    pTb = E1b(nc.sbuf_tensor("pTb", [128, 4, 128], BF16))
    ctx_sb = E1b(nc.sbuf_tensor("ctx_sb", [128, 2, D], BF16))
    ao_sb = E1b(nc.sbuf_tensor("ao_sb", [128, 2, D], F32))
    ctxT = E1b(nc.sbuf_tensor("ctxT", [128, 8, 128], BF16))

    rstd_free = [(None, 0)]

    def ln_stats(xap, pt, wait_ldx=None):
        if wait_ldx is not None:
            P.wait("vector", wait_ldx[0], wait_ldx[1])
        emit_inc("vector", lambda e, pt=pt, xap=xap: e.bn_stats(st6[:, pt, 0:6], xap[:, 0:512]))
        m2 = emit_inc("vector", lambda e, pt=pt, xap=xap: e.bn_stats(st6[:, pt, 6:12], xap[:, 512:1024]))
        P.wait("vector", s_ve, m2)
        return emit_inc("vector", lambda e, pt=pt: e.bn_aggr(
            mv[:, pt, 0:2], st6.ap()[:, pt, 0:12].rearrange("p (k t) -> p k t", t=3)))

    def vstep(fn):
        m = emit_inc("vector", fn)
        P.wait("vector", s_ve, m)
        return m

    def ln_smalls():
        if rstd_free[0][0] is not None:
            P.wait("vector", rstd_free[0][0], rstd_free[0][1])
        P.wait("vector", s_ve, s_ve.n)
        m1 = vstep(lambda e: e.tensor_scalar_add(veps[:], mv[:, :, 1], EPS))
        P.wait("scalar", s_ve, m1)
        ml = emit_inc("scalar", lambda e: e.activation(tmp_a[:], veps[:], AF.Ln))
        P.wait("scalar", s_se, ml)
        m2 = emit_inc("scalar", lambda e: e.activation(rstd[:], tmp_a[:], AF.Exp, scale=-0.5))
        P.wait("vector", s_se, m2)
        return vstep(lambda e: e.scalar_tensor_tensor(
            nmr[:], mv[:, :, 0], -1.0, rstd[:], OP.mult, OP.mult))

    def ln_apply(dst_ap, src_ap, p):
        m = emit_inc("scalar", lambda e, d=dst_ap, s=src_ap, p=p: e.activation(
            d, s, AF.Identity, bias=nmr[:, p:p + 1], scale=rstd[:, p:p + 1]))
        rstd_free[0] = (s_se, m)
        return m

    def transpose8(src_sb, evac_eng, dst_fn, extra_te_wait=None):
        if extra_te_wait is not None:
            P.wait("tensor", *extra_te_wait)
        for j in range(8):
            half, off = j // 4, (j % 4) * 128
            fn = (lambda e, j=j, half=half, off=off, s=src_sb: e.matmul(
                P01[:, half * 512 + off: half * 512 + off + 128],
                s[:, j * 128:(j + 1) * 128], ident[:], start=True, stop=True))
            if j == 7:
                tmark = emit_inc("tensor", fn)
            else:
                P.emit("tensor", fn)
        P.wait(evac_eng, s_te, tmark)
        op = "copy" if evac_eng == "scalar" else "tensor_copy"
        fn1 = lambda e: getattr(e, op)(
            dst_fn(0), P01.ap()[:, 0:512].rearrange("p (j d) -> p j d", d=128))
        fn2 = lambda e: getattr(e, op)(
            dst_fn(1), P01.ap()[:, 512:1024].rearrange("p (j d) -> p j d", d=128))
        P.emit(evac_eng, fn1)
        m = emit_inc(evac_eng, fn2)
        P.wait("tensor", inc_of(evac_eng), m)
        return m

    P.wait("vector", s_se, se_wt_all)
    P.wait("vector", s_te, te_boe2)
    emit_inc("vector", lambda e: e.memset(kT[:], 0.0))
    m_z = emit_inc("vector", lambda e: e.memset(vsl[:], 0.0))
    P.wait("vector", s_ve, m_z)
    ve_memset = emit_inc("vector", lambda e: e.memset(
        vsl.ap().rearrange("p s (h c) -> p s h c", c=65)[:, :, :, 64:65], 1.0))

    xln_free = [0, 0]
    xres_free_ve = [(None, 0), (None, 0)]
    ao_free = [0, 0]
    wo_ev_marks = [0, 0]
    xres_free_st = [0, 0]
    xld = [[0, 0] for _ in range(NT)]
    xrld = [[0, 0] for _ in range(NT)]
    yst = [[0, 0] for _ in range(NT)]
    proj_v_done = [0] * (NT + 1)
    vmm_last = [0]
    kq_evac_last = [0]
    ln1_sm = [0]
    wo_mark_last = [0]
    xn_ovl_mark = [0, 0]

    A = {"exp": 0, "mask": 0, "pT": 0, "pTa_ev": 0, "pTb_ev": 0, "ctx_ev": 0, "wo_ev": 0, "gg": 0, "exp_pair": [0, 0], "mask_pair": [0, 0], "pT_pair": [0, 0], "pend": None, "ctx_mm": 0}

    def emit_projLNa(t):
        flush_pT_ctx()
        par = t % 2
        if t == 0:
            P.wait("gpsimd", s_se, se_wt_all)
            P.wait("gpsimd", s_te, te_boe2)
        for p in range(2):
            P.wait("gpsimd", s_se, xln_free[p])
            P.wait("gpsimd", s_x[p], s_x[p].n)
            P.emit("gpsimd", lambda e, t=t, p=p: e.dma_start(
                out=x_ln[:, p, :], in_=x_d[t * TILE + p * PT: t * TILE + (p + 1) * PT, :]
            ).then_inc(s_x[p].h, 16))
            xld[t][p] = s_x[p].inc(16)
        for p in range(2):
            ln_stats(x_ln[:, p, :], p, wait_ldx=(s_x[p], xld[t][p]))
        ln1_sm[0] = ln_smalls()

    def emit_projLNb(t):
        flush_pT_ctx()
        par = t % 2
        P.wait("scalar", s_ve, ln1_sm[0])
        P.wait("tensor", s_se, A["pTb_ev"])
        P.wait("tensor", s_se, A["pTa_ev"])
        P.wait("tensor", s_se, kq_evac_last[0])
        P.wait("scalar", s_te, vmm_last[0])
        if t > 0:
            xn_ovl_mark[par] = emit_inc("scalar", lambda e, par=par, prv=(t - 1) % 2: e.copy(
                xnT.ap()[:, par, :, 0:32], xnT.ap()[:, prv, :, TILE: TILE + 32]))

        for p in range(2):
            ap_mark = ln_apply(xnb[:], x_ln[:, p, :], p)
            xln_free[p] = ap_mark
            xnT_ev = transpose8(
                xnb, "scalar",
                lambda h, p=p, par=par: xnT[:, par, 4 * h: 4 * h + 4,
                                            32 + p * PT: 32 + (p + 1) * PT],
                extra_te_wait=(s_se, ap_mark))
        if t == 0:
            P.wait("scalar", s_se, xnT_ev)
            xn_ovl_mark[par] = emit_inc("scalar", lambda e, par=par: e.copy(
                xnT.ap()[:, par, :, 0:32], xnT.ap()[:, par, :, 32:64]))

    def emit_projMM(t):
        flush_pT_ctx()
        kbuf = t % 2
        par = t % 2
        if t == 0:
            P.wait("tensor", s_wst, marks["Wq"][1])
            P.wait("tensor", s_se, se_wt_all)
            P.wait("tensor", s_se, se_bq)
            P.wait("tensor", s_ve, ve_boe)
            P.wait("tensor", s_ve, ve_memset)

        P.wait("tensor", s_se, A["exp"])
        KQB = [(P01, 0), (P01, 512), (P23, 0), (P23, 512)]
        q_free = [0, 0, 0, 0]
        for wi_, (WT_, isq) in enumerate(((WqT, True), (WkT, False))):
            for j in range(8):
                h = j % 4
                PB, off = KQB[h]
                P.wait("tensor", s_se, q_free[h])
                for d in range(8):
                    fn = (lambda e, j=j, d=d, par=par, WT_=WT_, PB=PB, off=off: e.matmul(
                        PB[:, off: off + TILE], WT_[:, d, j * 128:(j + 1) * 128],
                        xnT[:, par, d, 32:32 + TILE], start=(d == 0), stop=(d == 7)))
                    if d == 7:
                        qmm = emit_inc("tensor", fn)
                    else:
                        P.emit("tensor", fn)
                P.wait("scalar", s_te, qmm)
                if isq:
                    q_free[h] = emit_inc("scalar", lambda e, j=j, par=par, PB=PB, off=off: e.activation(
                        qT[:, par, j, :], PB[:, off: off + TILE], AF.Identity,
                        bias=bq_fm[:, j:j + 1]))
                else:
                    P.wait("scalar", s_ve, ve_memset)
                    q_free[h] = emit_inc("scalar", lambda e, j=j, kb=kbuf, PB=PB, off=off: e.copy(
                        kT[:, kb, j, :], PB[:, off: off + TILE]))
        for h in range(4):
            P.wait("tensor", s_se, q_free[h])
        kq_evac_last[0] = max(q_free)

        P.wait("tensor", s_ve, A["ctx_ev"])
        P.wait("tensor", s_se, A["exp"])
        P.wait("tensor", s_se, xn_ovl_mark[t % 2])
        vev_last = 0
        slv = vsl.ap().rearrange("p s (h c) -> p s h c", c=65)
        par2 = t % 2
        slot_specs = [(2 * t, P23, 0), (2 * t + 1, P45, 128)]
        for sl, pt_, c0 in slot_specs:
            for half in range(2):
                for d in range(8):
                    fn = (lambda e, h=half, d=d, pt_=pt_, c0=c0, par=par2: e.matmul(
                        pt_[:, h * 512:(h + 1) * 512],
                        xnT[:, par, d, c0: c0 + 128],
                        WvT[:, d, h * 512:(h + 1) * 512],
                        start=(d == 0), stop=(d == 7)))
                    if half == 1 and d == 7:
                        vp_mark = emit_inc("tensor", fn)
                    else:
                        P.emit("tensor", fn)
            P.wait("vector", s_te, vp_mark)
            P.emit("vector", lambda e, sl=sl, pt_=pt_, slv=slv: e.tensor_copy(
                slv[:, sl % 6, 0:8, 0:64],
                pt_.ap()[:, 0:512].rearrange("p (h c) -> p h c", c=64)))
            vev_last = emit_inc("vector", lambda e, sl=sl, pt_=pt_, slv=slv: e.tensor_copy(
                slv[:, sl % 6, 8:16, 0:64],
                pt_.ap()[:, 512:1024].rearrange("p (h c) -> p h c", c=64)))
            P.wait("tensor", s_ve, vev_last)
        if t == NT - 1:
            P.wait("tensor", s_ve, vev_last)
            for half in range(2):
                for d in range(8):
                    fn = (lambda e, h=half, d=d, par=par2: e.matmul(
                        P23[0:32, h * 512:(h + 1) * 512],
                        xnT[:, par, d, TILE: TILE + 32],
                        WvT[:, d, h * 512:(h + 1) * 512],
                        start=(d == 0), stop=(d == 7)))
                    if half == 1 and d == 7:
                        vp_mark = emit_inc("tensor", fn)
                    else:
                        P.emit("tensor", fn)
            P.wait("vector", s_te, vp_mark)
            sl = 2 * t + 2
            P.emit("vector", lambda e, sl=sl, slv=slv: e.tensor_copy(
                slv[0:32, sl % 6, 0:8, 0:64],
                P23.ap()[0:32, 0:512].rearrange("p (h c) -> p h c", c=64)))
            vev_last = emit_inc("vector", lambda e, sl=sl, slv=slv: e.tensor_copy(
                slv[0:32, sl % 6, 8:16, 0:64],
                P23.ap()[0:32, 512:1024].rearrange("p (h c) -> p h c", c=64)))
            P.wait("tensor", s_ve, vev_last)
        vmm_last[0] = vp_mark
        proj_v_done[t] = vev_last

    def pend_pT(ppar):
        P.wait("tensor", s_se, A["exp"])
        P.wait("tensor", s_se, A["pTa_ev"])
        P.wait("tensor", s_se, A["pTb_ev"])
        for idx in range(4):
            fn = (lambda e, idx=idx, ppar=ppar: (
                e.matmul(P01[:, idx * 128:(idx + 1) * 128],
                         probs[:, ppar, idx, 0:128], ident[:], start=True, stop=True),
                e.matmul(P01[0:64, 512 + idx * 128: 512 + (idx + 1) * 128],
                         probs[:, ppar, idx, 128:192], ident[:], start=True, stop=True))[-1])
            if idx == 3:
                A["pT"] = emit_inc("tensor", fn)
            else:
                P.emit("tensor", fn)
        A["pT_pair"][ppar] = A["pT"]

    def pend_pTa_ev():
        P.wait("scalar", s_te, A["pT"])
        A["pTa_ev"] = emit_inc("scalar", lambda e: e.copy(
            pTa.ap().rearrange("p a b -> p (a b)"), P01[:, 0:512]))

    def pend_pTb_ev():
        P.wait("scalar", s_te, A["pT"])
        A["pTb_ev"] = emit_inc("scalar", lambda e: e.copy(
            pTb.ap()[0:64].rearrange("p a b -> p (a b)"), P01[0:64, 512:1024]))

    def pend_ctx(g, b, B):
        P.wait("tensor", s_se, A["pTa_ev"])
        P.wait("tensor", s_se, A["pTb_ev"])
        P.wait("tensor", s_ve, A["ctx_ev"])
        sl0, sl1 = B % 6, (B + 1) % 6
        for idx in range(4):
            hd_ = 4 * g + HEAD_OF[idx]
            fn = (lambda e, idx=idx, hd_=hd_, sl0=sl0, sl1=sl1: (
                e.matmul(P45[:, 512 + idx * 128: 512 + idx * 128 + 65],
                         pTa[:, idx, :], vsl[:, sl0, hd_ * 65: hd_ * 65 + 65],
                         start=True, stop=False),
                e.matmul(P45[:, 512 + idx * 128: 512 + idx * 128 + 65],
                         pTb[0:64, idx, :], vsl[0:64, sl1, hd_ * 65: hd_ * 65 + 65],
                         start=False, stop=True))[-1])
            if idx == 3:
                A["ctx_mm"] = emit_inc("tensor", fn)
            else:
                P.emit("tensor", fn)

    def pend_ctx_ev(g, b):
        P.wait("vector", s_te, A["ctx_mm"])
        m_rc = emit_inc("vector", lambda e: e.reciprocal(
            recip[:, 0:4],
            P45.ap()[:, 512:1024].rearrange("p (i c) -> p i c", c=128)[:, 0:4, 64:65]))
        P.wait("vector", s_ve, m_rc)
        cview = (ctx_sb.ap()[:, b, 256 * g: 256 * (g + 1)]
                 .rearrange("p (u v c) -> p v u c", v=2, c=64))
        psrc = (P45.ap()[:, 512:1024].rearrange("p (i c) -> p i c", c=128)[:, 0:4, 0:64]
                .rearrange("p (a b) c -> p a b c", a=2))
        A["ctx_ev"] = emit_inc("vector", lambda e, cv=cview, ps=psrc: e.tensor_tensor(
            cv, ps,
            recip.ap().rearrange("p (a b) -> p a b", a=2).unsqueeze(3)
            .broadcast_to([128, 2, 2, 64]), op=OP.mult))

    def flush_pT_ctx():
        if A["pend"] is None:
            return
        g, b, B, ppar = A["pend"]
        A["pend"] = None
        pend_pT(ppar)
        pend_pTa_ev()
        pend_pTb_ev()
        pend_ctx(g, b, B)
        pend_ctx_ev(g, b)

    def emit_attn_b(t, b):
        par_t = t % 2
        P.wait("tensor", s_ve, proj_v_done[min(t + b, NT - 1)])
        B = 2 * t + b
        g0 = A["gg"]
        A["gg"] += 4
        scmarks = {}

        def scores_of(g):
            gpar = (g0 + g) % 2
            SPP = P67 if gpar == 0 else P23
            P.wait("tensor", s_se, A["exp_pair"][gpar])
            if gpar == 0:
                P.wait("tensor", s_ve, A["wo_ev"])
            for idx in range(4):
                hd_ = 4 * g + HEAD_OF[idx]
                rb = (hd_ % 2) * 64
                pc = (hd_ % 2) * 512 + (idx % 2) * 192
                j = hd_ // 2
                qsl = (lambda rb=rb, j=j, b=b, par=par_t:
                       qT[rb:rb + 64, par, j, b * 128: b * 128 + 128])
                if b == 0:
                    kbp, kbc = (t - 1) % 2, t % 2
                    fn = (lambda e, rb=rb, pc=pc, j=j, kbp=kbp, kbc=kbc, q=qsl, SPP=SPP: (
                        e.matmul(SPP[:, pc: pc + 32], q(),
                                 kT[rb:rb + 64, kbp, j, 224:256], start=True, stop=True),
                        e.matmul(SPP[:, pc + 32: pc + 192], q(),
                                 kT[rb:rb + 64, kbc, j, 0:160], start=True, stop=True))[-1])
                else:
                    kbc, kbn = t % 2, (t + 1) % 2
                    fn = (lambda e, rb=rb, pc=pc, j=j, kbc=kbc, kbn=kbn, q=qsl, SPP=SPP: (
                        e.matmul(SPP[:, pc: pc + 160], q(),
                                 kT[rb:rb + 64, kbc, j, 96:256], start=True, stop=True),
                        e.matmul(SPP[:, pc + 160: pc + 192], q(),
                                 kT[rb:rb + 64, kbn, j, 0:32], start=True, stop=True))[-1])
                if idx == 3:
                    scmarks[g] = emit_inc("tensor", fn)
                else:
                    P.emit("tensor", fn)

        scores_of(0)
        for g in range(4):
            gpar = (g0 + g) % 2
            ppar = 0
            SPP = P67 if gpar == 0 else P23
            pend = A["pend"]
            A["pend"] = None
            if pend is not None:
                pend_pT(pend[3])
            P.wait("vector", s_te, scmarks[g])
            P.wait("vector", s_se, A["exp_pair"][gpar])
            if B == 0 or B == NSB - 1:
                c0_ = 0 if B == 0 else 160
                bm = 0
                for hh in range(2):
                    for ss in range(2):
                        off = hh * 512 + ss * 192 + c0_
                        bm = emit_inc("vector", lambda e, SPP=SPP, off=off: e.memset(
                            SPP[:, off: off + 32], -1e30))
                P.wait("vector", s_ve, bm)
            pm = 0
            for hh in range(2):
                pm = emit_inc("vector", lambda e, SPP=SPP, hh=hh: e.tensor_tensor(
                    SPP[:, hh * 512: hh * 512 + 384].rearrange("p (s c) -> p s c", c=192),
                    SPP[:, hh * 512: hh * 512 + 384].rearrange("p (s c) -> p s c", c=192),
                    masks.ap()[:, 0, :].unsqueeze(1).broadcast_to([128, 2, 192]),
                    op=OP.add))
            P.wait("vector", s_ve, pm)
            nm_mark = emit_inc("vector", lambda e, gpar=gpar, SPP=SPP: e.reduce_max(
                negmax[:, gpar, 0:1],
                SPP.ap().rearrange("p (h c) -> p h c", c=512)[:, :, 0:384],
                axis=AX.XY, negate=True))
            if g + 1 < 4:
                scores_of(g + 1)
            if pend is not None:
                pend_pTa_ev()
                pend_pTb_ev()
            P.wait("scalar", s_ve, nm_mark)
            P.wait("scalar", s_te, A["pT_pair"][ppar])
            A["exp"] = emit_inc("scalar", lambda e, gpar=gpar, SPP=SPP, ppar=ppar: e.activation(
                probs.ap()[:, 0, :, :].rearrange("p (a b) c -> p a (b c)", a=2),
                SPP.ap().rearrange("p (h c) -> p h c", c=512)[:, :, 0:384],
                AF.Exp, bias=negmax[:, gpar, 0:1]))
            A["exp_pair"][gpar] = A["exp"]
            if pend is not None:
                pend_ctx(pend[0], pend[1], pend[2])
            if pend is not None:
                pend_ctx_ev(pend[0], pend[1])
            A["pend"] = (g, b, B, ppar)

    def emit_postA(t):
        flush_pT_ctx()
        for p in range(2):
            if xres_free_ve[p][0] is not None:
                P.wait("gpsimd", xres_free_ve[p][0], xres_free_ve[p][1])
            P.wait("gpsimd", s_ys[p], xres_free_st[p])
            P.wait("gpsimd", s_xr[p], s_xr[p].n)
            P.emit("gpsimd", lambda e, t=t, p=p: e.dma_start(
                out=x_res[:, p, :], in_=x_d[t * TILE + p * PT: t * TILE + (p + 1) * PT, :]
            ).then_inc(s_xr[p].h, 16))
            xrld[t][p] = s_xr[p].inc(16)
        P.wait("tensor", s_se, A["pTb_ev"])
        for p in range(2):
            transpose8(
                ctx_sb.ap()[:, p, :], "scalar",
                lambda h: ctxT[:, 4 * h: 4 * h + 4, :],
                extra_te_wait=(s_ve, A["ctx_ev"]))
            if p == 0:
                P.wait("tensor", s_se, A["exp"])
            P.wait("tensor", s_ve, A["wo_ev"])
            for half in range(2):
                for j in range(8):
                    fn = (lambda e, h=half, j=j: e.matmul(
                        P67[:, h * 512:(h + 1) * 512], ctxT[:, j, :],
                        WoT[:, j, h * 512:(h + 1) * 512], start=(j == 0), stop=(j == 7)))
                    if half == 1 and j == 7:
                        wo_mark = emit_inc("tensor", fn)
                    else:
                        P.emit("tensor", fn)
            wo_mark_last[0] = wo_mark
            P.wait("vector", s_te, wo_mark)
            P.wait("vector", s_ve, ao_free[p])
            A["wo_ev"] = emit_inc("vector", lambda e, p=p: e.tensor_add(
                ao_sb[:, p, :], P67.ap()[:, 0:1024], boe_repl[:]))
            wo_ev_marks[p] = A["wo_ev"]

    def emit_postB(t):
        for p in range(2):
            ln_stats(ao_sb[:, p, :], p, wait_ldx=(s_ve, wo_ev_marks[p]))
        sm = ln_smalls()
        for p in range(2):
            P.wait("vector", s_ve, sm)
            P.wait("vector", s_xr[p], xrld[t][p])
            vstep(lambda e, p=p: e.tensor_scalar_add(
                x_res[:, p, :], x_res[:, p, :], nmr[:, p:p + 1]))
            y_mark = vstep(lambda e, p=p: e.scalar_tensor_tensor(
                x_res[:, p, :], ao_sb[:, p, :], rstd[:, p:p + 1], x_res[:, p, :],
                OP.mult, OP.add))
            rstd_free[0] = (s_ve, y_mark)
            xres_free_ve[p] = (s_ve, y_mark)
            ao_free[p] = y_mark
            P.wait("sync", s_ve, y_mark)
            P.wait("sync", s_ys[p], s_ys[p].n)
            P.emit("sync", lambda e, t=t, p=p: e.dma_start(
                out=y_d[t * TILE + p * PT: t * TILE + (p + 1) * PT, :], in_=x_res[:, p, :]
            ).then_inc(s_ys[p].h, 16))
            yst[t][p] = s_ys[p].inc(16)
            xres_free_st[p] = yst[t][p]

    emit_projLNa(0)
    emit_projLNb(0)
    emit_projMM(0)
    emit_projLNa(1)
    emit_projLNb(1)
    emit_projMM(1)
    emit_projLNa(2)
    emit_projLNb(2)
    emit_attn_b(0, 0)
    for k in range(NT):
        emit_attn_b(k, 1)
        emit_postA(k)
        if k + 1 < NT:
            emit_attn_b(k + 1, 0)
        emit_postB(k)
        if k + 3 < NT:
            emit_projLNa(k + 3)
        if k + 2 < NT:
            emit_projMM(k + 2)
        if k + 3 < NT:
            emit_projLNb(k + 3)

    es1b.close()
    es1.close()
    es2 = ExitStack()
    E2 = es2.enter_context
    w2b = E2(nc.sbuf_tensor("w2b", [128, 32, D], BF16))
    b2_row = E2(nc.sbuf_tensor("b2_row", [1, D], F32))
    b2_repl = E2(nc.sbuf_tensor("b2_repl", [128, D], F32))
    y_sb = E2(nc.sbuf_tensor("y_sb", [128, 2, 2, D], F32))
    ynb = E2(nc.sbuf_tensor("ynb", [128, D], BF16))
    ynT = E2(nc.sbuf_tensor("ynT", [128, 2, 8, TILE], BF16))
    hb = E2(nc.sbuf_tensor("hb", [128, 32, TILE], BF16))
    z_sb = E2(nc.sbuf_tensor("z_sb", [128, 2, D], F32))
    P.wait("gpsimd", s_se, xln_free[0])
    P.wait("gpsimd", s_se, xln_free[1])
    gload("b2", b2_row[:], b2_d, sem=s_b2g)
    P.wait("gpsimd", s_te, wo_mark_last[0])
    gload("w2", w2b[:], w2_d.rearrange("(c p) d -> p c d", p=128), sem=s_w2)

    y_free_ve = [[0, 0], [0, 0]]
    y_free_st = [[0, 0], [0, 0]]
    hb_free = [0]
    relu_marks = []

    P.wait("tensor", s_w1, 16)
    P.wait("tensor", s_b2g, 16)
    P.wait("scalar", s_b1g, 32)
    P.wait("vector", s_b1g, 32)
    P.wait("tensor", s_ve, A["ctx_ev"])
    P.wait("tensor", s_ve, A["wo_ev"])
    P.wait("tensor", s_se, s_se.n)
    for half in range(2):
        fn = (lambda e, h=half: e.matmul(
            P45[:, h * 512:(h + 1) * 512], ones_row[0:1, 0:128],
            b2_row[0:1, h * 512:(h + 1) * 512], start=True, stop=True))
        if half == 1:
            m_b2 = emit_inc("tensor", fn)
        else:
            P.emit("tensor", fn)
    P.wait("vector", s_te, m_b2)
    P.wait("vector", s_ys[0], s_ys[0].n)
    P.wait("vector", s_ys[1], s_ys[1].n)
    m_b2r = emit_inc("vector", lambda e: e.tensor_copy(b2_repl[:], P45[:, 0:1024]))
    P.wait("tensor", s_ve, m_b2r)

    yld = [[0, 0] for _ in range(NT)]
    ln3_sm = [0]
    ln3_marks = [0, 0]
    zev_prev = [0, 0]

    def p2_load(t):
        par = t % 2
        for p in range(2):
            P.wait("sync", s_ve, y_free_ve[par][p])
            P.wait("sync", s_os[p], y_free_st[par][p])
            P.wait("sync", s_ys[p], s_ys[p].n if t < 2 else yst[t][p])
            P.wait("sync", s_yl[p], s_yl[p].n)
            P.emit("sync", lambda e, t=t, p=p, par=par: e.dma_start(
                out=y_sb[:, par, p, :], in_=y_d[t * TILE + p * PT: t * TILE + (p + 1) * PT, :]
            ).then_inc(s_yl[p].h, 16))
            yld[t][p] = s_yl[p].inc(16)

    def p2_stats(t):
        par = t % 2
        for p in range(2):
            ln_stats(y_sb[:, par, p, :], p, wait_ldx=(s_yl[p], yld[t][p]))
        ln3_sm[0] = ln_smalls()

    def p2_trans_p(t, p):
        par = t % 2
        bank = P45 if p == 0 else P67
        P.wait("scalar", s_ve, ln3_sm[0])
        ap_mark = ln_apply(ynb[:], y_sb[:, par, p, :], p)
        P.wait("tensor", s_ve, zev_prev[p])
        P.wait("tensor", s_se, ln3_marks[p])
        P.wait("tensor", s_se, ap_mark)
        for j in range(8):
            half, off = j // 4, (j % 4) * 128
            fn = (lambda e, j=j, half=half, off=off, bank=bank: e.matmul(
                bank[:, half * 512 + off: half * 512 + off + 128],
                ynb[:, j * 128:(j + 1) * 128], ident[:], start=True, stop=True))
            if j == 7:
                tmark = emit_inc("tensor", fn)
            else:
                P.emit("tensor", fn)
        P.wait("scalar", s_te, tmark)
        P.emit("scalar", lambda e, p=p, par=par, bank=bank: e.copy(
            ynT[:, par, 0:4, p * PT:(p + 1) * PT],
            bank.ap()[:, 0:512].rearrange("p (j d) -> p j d", d=128)))
        ln3_marks[p] = emit_inc("scalar", lambda e, p=p, par=par, bank=bank: e.copy(
            ynT[:, par, 4:8, p * PT:(p + 1) * PT],
            bank.ap()[:, 512:1024].rearrange("p (j d) -> p j d", d=128)))

    p2_load(0)
    p2_stats(0)
    p2_trans_p(0, 0)
    p2_trans_p(0, 1)
    P.wait("tensor", s_se, ln3_marks[0])
    P.wait("tensor", s_se, ln3_marks[1])
    for t in range(NT):
        par = t % 2
        region_free = [(s_te, 0), (s_te, 0)]
        for cpair in range(16):
            if t + 1 < NT:
                if cpair == 0:
                    p2_load(t + 1)
                elif cpair == 6:
                    p2_stats(t + 1)
                elif cpair == 10:
                    p2_trans_p(t + 1, 0)
                elif cpair == 12:
                    p2_trans_p(t + 1, 1)
            reg = cpair % 2
            ptr = P01 if reg == 0 else P23
            P.wait("tensor", *region_free[reg])
            for ci in range(2):
                c = 2 * cpair + ci
                for d in range(8):
                    fn = (lambda e, c=c, ci=ci, d=d, ptr=ptr, par=par: e.matmul(
                        ptr[:, ci * 512: ci * 512 + TILE], w1b[:, d, c * 128:(c + 1) * 128],
                        ynT[:, par, d, :], start=(d == 0), stop=(d == 7)))
                    if ci == 1 and d == 7:
                        h_mark = emit_inc("tensor", fn)
                    else:
                        P.emit("tensor", fn)
            c0 = 2 * cpair
            if cpair % 2 == 0:
                P.wait("scalar", s_te, h_mark)
                P.wait("scalar", s_te, hb_free[0])
                P.emit("scalar", lambda e, c0=c0, ptr=ptr: e.activation(
                    hb[:, c0, :], ptr[:, 0:TILE], AF.Relu, bias=b1T[:, c0:c0 + 1]))
                m = emit_inc("scalar", lambda e, c0=c0, ptr=ptr: e.activation(
                    hb[:, c0 + 1, :], ptr[:, 512:512 + TILE], AF.Relu, bias=b1T[:, c0 + 1:c0 + 2]))
                region_free[reg] = (s_se, m)
            else:
                P.wait("vector", s_te, h_mark)
                P.wait("vector", s_te, hb_free[0])
                P.emit("vector", lambda e, c0=c0, ptr=ptr: e.tensor_scalar(
                    hb[:, c0, :], ptr[:, 0:TILE], b1T[:, c0:c0 + 1], 0.0, OP.add, OP.max))
                m = emit_inc("vector", lambda e, c0=c0, ptr=ptr: e.tensor_scalar(
                    hb[:, c0 + 1, :], ptr[:, 512:512 + TILE], b1T[:, c0 + 1:c0 + 2], 0.0, OP.add, OP.max))
                region_free[reg] = (s_ve, m)
        P.wait("tensor", *region_free[0])
        P.wait("tensor", *region_free[1])
        P.wait("tensor", s_w2, 16)
        z_ev_marks = []
        for ts in range(2):
            preg = P45 if ts == 0 else P67
            P.wait("tensor", s_se, ln3_marks[ts])
            P.wait("tensor", s_ve, zev_prev[ts])
            for half in range(2):
                for c in range(32):
                    fn = (lambda e, ts=ts, h=half, c=c, preg=preg: e.matmul(
                        preg[:, h * 512:(h + 1) * 512], hb[:, c, ts * PT:(ts + 1) * PT],
                        w2b[:, c, h * 512:(h + 1) * 512], start=(c == 0), stop=(c == 31)))
                    if half == 1 and c == 31:
                        z_mark = emit_inc("tensor", fn)
                    else:
                        P.emit("tensor", fn)
            P.wait("vector", s_te, z_mark)
            if rstd_free[0][0] is not None:
                P.wait("vector", rstd_free[0][0], rstd_free[0][1])
            z_ev_marks.append(emit_inc("vector", lambda e, ts=ts, preg=preg: e.tensor_add(
                z_sb[:, ts, :], preg.ap()[:, 0:1024], b2_repl[:])))
        zev_prev[0], zev_prev[1] = z_ev_marks[0], z_ev_marks[1]
        hb_free[0] = s_te.n
        for p in range(2):
            ln_stats(z_sb[:, p, :], p, wait_ldx=(s_ve, z_ev_marks[p]))
        sm = ln_smalls()
        for p in range(2):
            P.wait("vector", s_ve, sm)
            vstep(lambda e, p=p, par=par: e.tensor_scalar_add(
                y_sb[:, par, p, :], y_sb[:, par, p, :], nmr[:, p:p + 1]))
            o_mark = vstep(lambda e, p=p, par=par: e.scalar_tensor_tensor(
                y_sb[:, par, p, :], z_sb[:, p, :], rstd[:, p:p + 1], y_sb[:, par, p, :],
                OP.mult, OP.add))
            rstd_free[0] = (s_ve, o_mark)
            y_free_ve[par][p] = o_mark
            P.wait("sync", s_ve, o_mark)
            P.wait("sync", s_os[p], s_os[p].n)
            P.emit("sync", lambda e, t=t, p=p, par=par: e.dma_start(
                out=out_d[t * TILE + p * PT: t * TILE + (p + 1) * PT, :], in_=y_sb[:, par, p, :]
            ).then_inc(s_os[p].h, 16))
            y_free_st[par][p] = s_os[p].inc(16)

    P.wait("sync", s_os[0], s_os[0].n)
    P.wait("sync", s_os[1], s_os[1].n)

    with nc.Block() as block:
        block.sync(lambda e: [f(e) for f in P.streams["sync"]])
        block.gpsimd(lambda e: [f(e) for f in P.streams["gpsimd"]])
        block.vector(lambda e: [f(e) for f in P.streams["vector"]])
        block.scalar(lambda e: [f(e) for f in P.streams["scalar"]])
        block.tensor(lambda e: [f(e) for f in P.streams["tensor"]])
    es2.close()
    es.close()
    return nc


def make_consts():
    ident = np.eye(128, dtype=np.float32)
    ident8 = (0.125 * np.eye(128)).astype(np.float32)
    i = np.arange(128)[:, None]
    j = np.arange(192)[None, :]
    band = ((j - i >= 0) & (j - i <= 64)).astype(np.float32)
    maskf = ((1.0 - band) * -1e30).astype(np.float32)
    maskl = band * (j < 160).astype(np.float32)
    return ident, ident8, maskf, band, maskl


def kernel(**inputs):
    x = np.ascontiguousarray(np.asarray(inputs["L0_attn_x"], dtype=np.float32))
    ident, ident8, maskf, maskm, maskl = make_consts()
    base = {
        "Wq": np.ascontiguousarray(np.asarray(inputs["L0_attn_Wq"], dtype=np.float32)),
        "Wk": np.ascontiguousarray(np.asarray(inputs["L0_attn_Wk"], dtype=np.float32)),
        "Wv": np.ascontiguousarray(np.asarray(inputs["L0_attn_Wv"], dtype=np.float32)),
        "Wo": np.ascontiguousarray(np.asarray(inputs["L0_attn_Wo"], dtype=np.float32)),
        "bq": np.ascontiguousarray(np.asarray(inputs["L0_attn_bq"], dtype=np.float32)).reshape(1, D),
        "bv": np.ascontiguousarray(np.asarray(inputs["L0_attn_bv"], dtype=np.float32)).reshape(1, D),
        "bo": np.ascontiguousarray(np.asarray(inputs["L0_attn_bo"], dtype=np.float32)).reshape(1, D),
        "w1": np.ascontiguousarray(np.asarray(inputs["L0_ffn_w1"], dtype=np.float32)),
        "b1": np.ascontiguousarray(np.asarray(inputs["L0_ffn_b1"], dtype=np.float32)).reshape(1, W1H),
        "w2": np.ascontiguousarray(np.asarray(inputs["L0_ffn_w2"], dtype=np.float32)),
        "b2": np.ascontiguousarray(np.asarray(inputs["L0_ffn_b2"], dtype=np.float32)).reshape(1, D),
        "ident": ident, "ident8": ident8,
        "maskf": maskf, "maskm": maskm, "maskl": maskl,
    }
    nc = bass.Bass("TRN2", target_bir_lowering=False)
    build(nc)
    in_maps = [dict(base, x=np.ascontiguousarray(x[b])) for b in range(8)]
    res = run_bass_kernel_spmd(nc, in_maps, core_ids=list(range(8)))
    return np.stack([np.asarray(r["out"]) for r in res.results], axis=0)
